# version 11 (final)
# baseline (speedup 1.0000x reference)
import contextlib
import numpy as np
import ml_dtypes
import concourse.bass as bass
import concourse.mybir as mybir
from concourse.bass_utils import run_bass_kernel_spmd

F32 = mybir.dt.float32
BF16 = mybir.dt.bfloat16
U32 = mybir.dt.uint32
AF = mybir.ActivationFunctionType
ALU = mybir.AluOpType
AX = mybir.AxisListType

S, D = 4096, 1024
HEADS, HDIM = 16, 64
LEVELS, HID = 3, 4096
NEXP, TOPK, FF, CAP = 16, 2, 1408, 5120
NTOK = 512
NT = S // NTOK
NSUB = 4
EPS = 1e-5

DEBUG_TAPS = ()
BUILD_PHASES = 99


def build_program(taps=(), phases=99):
    nc = bass.Bass("TRN2", target_bir_lowering=False)

    def din(name, shape, dtype=F32):
        return nc.dram_tensor(name, shape, dtype, kind="ExternalInput").ap()

    x_h = din("x", [S, D])
    wq_h = din("wq", [D, D]); wk_h = din("wk", [D, D])
    wv_h = din("wv", [D, D]); wo_h = din("wo", [D, D])
    ln1g_h = din("ln1g", [D]); ln1b_h = din("ln1b", [D])
    ln2g_h = din("ln2g", [D]); ln2b_h = din("ln2b", [D])
    cw1_h = din("cw1", [LEVELS, D, HID]); cb1_h = din("cb1", [LEVELS, HID])
    cw2_h = din("cw2", [LEVELS, HID, D]); cb2_h = din("cb2", [LEVELS, D])
    rw_h = din("rw", [D, NEXP]); rb_h = din("rb", [NEXP])
    wg_h = din("wg", [NEXP, D, FF]); wu_h = din("wu", [NEXP, D, FF])
    wd_h = din("wd", [NEXP, FF, D])
    swg_h = din("swg", [D, FF]); swu_h = din("swu", [D, FF]); swd_h = din("swd", [FF, D])
    wgo_h = din("wgo", [2, D, FF]); wuo_h = din("wuo", [2, D, FF]); wdo_h = din("wdo", [2, FF, D])
    halfpar_h = din("halfpar", [1, 2 * NEXP])
    rankvec_h = din("rankvec_h", [128, 1])
    identb_h = din("identb_h", [128, 128], BF16)
    identf_h = din("identf_h", [128, 128], F32)
    maskt_h = din("maskt_h", [128, 128], F32)
    trius_h = din("trius_h", [128, 128], BF16)
    rankpre_h = din("rankpre_h", [8, 1], F32)

    y_h = nc.dram_tensor("y", [S, D], F32, kind="ExternalOutput").ap()

    def scrt(name, shape, dtype):
        kind = "ExternalOutput" if name in taps else "Internal"
        return nc.dram_tensor(name, shape, dtype, kind=kind).ap()

    sp_xa = scrt("sp_xa", [S, D], F32)
    sp_h = scrt("sp_h", [S, D], BF16)
    sp_hA = scrt("sp_hA", [S, D], BF16)
    sp_hB = scrt("sp_hB", [S, D], BF16)
    sp_h3 = scrt("sp_h3", [S, D], F32)
    sp_x2 = scrt("sp_x2", [S, D], F32)
    sp_z2 = scrt("sp_z2", [S, D], F32)
    sp_z2fm = scrt("sp_z2fm", [D, S], BF16)
    sp_acc = scrt("sp_acc", [S, D], BF16)
    sp_z2tm = scrt("sp_z2tm", [S, D], BF16)
    z2_all = nc.dram_tensor("z2_all", [8 * S, D], BF16, kind="Internal").ap()
    t_c = nc.dram_tensor("t_c", [16 * 128 * 40, 1], F32, kind="Internal").ap()
    t_own = nc.dram_tensor("t_own", [2 * 128 * 40, 1], F32, kind="Internal").ap()
    outA = nc.dram_tensor("outA", [CAP, D], BF16, kind="Internal").ap()
    outB = nc.dram_tensor("outB", [CAP, D], BF16, kind="Internal").ap()
    all2 = nc.dram_tensor("all2", [16 * CAP + 128, D], BF16, kind="Internal").ap()
    cc_in = nc.dram_tensor("cc_in", [1, NEXP], F32, kind="Internal").ap()
    cc_out = nc.dram_tensor("cc_out", [8, NEXP], F32, kind="Internal").ap()
    tap_small = scrt("tap_small", [8, 64], F32)

    with contextlib.ExitStack() as st_:
        sem = {}
        cnt = {}
        for sname in ("qs", "qp", "qa", "qw", "pe", "act", "dve", "pool", "cc"):
            sem[sname] = st_.enter_context(nc.semaphore(sname))
            cnt[sname] = 0

        def inc(inst, s, k=1):
            cnt[s] += k
            inst.then_inc(sem[s], k)
            return (s, cnt[s])

        def now(s):
            return (s, cnt[s])

        def wge(eng, *evs):
            for ev in evs:
                if ev is not None and ev[1] > 0:
                    if ev[0] in ("qs", "qp", "qa", "qw"):
                        eng.wait_ge(sem[ev[0]], cnt[ev[0]])
                    else:
                        eng.wait_ge(sem[ev[0]], ev[1])

        _uid = [0]

        def _un(name):
            _uid[0] += 1
            return f"{name}_{_uid[0]}"

        sb = lambda name, shape, dt=F32: st_.enter_context(nc.sbuf_tensor(name, shape, dt))
        identb = sb("identb", [128, 128], BF16)
        identf = sb("identf", [128, 128], F32)
        maskt = sb("maskt", [128, 128], F32)
        trius = sb("trius", [128, 128], BF16)
        rankpre = sb("rankpre", [8, 1], F32)
        consts = sb("consts", [128, 128], F32)
        b2bc = sb("b2bc", [128, LEVELS * D], BF16)
        rbbc = sb("rbbc", [128, NEXP], F32)
        iota16 = sb("iota16", [128, NEXP], F32)
        epsv = sb("epsv", [128, 1], F32)
        onesf = sb("onesf", [1, 128], F32)
        onesbc = sb("onesbc", [128, 1], BF16)
        onesgc = sb("onesgc", [128, 1], F32)
        zerosb = sb("zerosb", [128, 512], BF16)
        rw_sb = sb("rw_sb", [128, 8 * NEXP], BF16)
        oh1s = sb("oh1s", [128, 32 * NEXP], BF16)
        oh2s = sb("oh2s", [128, 32 * NEXP], BF16)
        ohsumb = sb("ohsumb", [128, 32 * NEXP], BF16)
        gstore = sb("gstore", [128, 64], F32)
        gwst = sb("gwst", [128, 64], F32)
        halfbc = sb("halfbc", [128, NEXP], F32)
        parbc = sb("parbc", [128, NEXP], F32)
        iotacol = sb("iotacol", [128, 1], F32)
        oobcol = sb("oobcol", [128, 1], F32)
        zrowcol = sb("zrowcol", [128, 1], F32)
        rankvec = sb("rankvec", [128, 1], F32)
        sidxu = sb("sidxu", [128, 64], U32)
        cidxu = sb("cidxu", [128, 64], U32)
        tokidu = sb("tokidu", [128, 32], U32)
        cnts_all = sb("cnts_all", [8, NEXP], F32)
        offs = sb("offs", [1, NEXP], F32)
        base = sb("base", [1, NEXP], F32)
        cntrow = sb("cntrow", [1, NEXP], F32)
        scr16 = sb("scr16", [128, 10 * NEXP], F32)
        pos1 = sb("pos1", [128, 1], F32)
        pos2 = sb("pos2", [128, 1], F32)
        bv1 = sb("bv1", [128, 1], F32)
        bv2 = sb("bv2", [128, 1], F32)
        w1c = sb("w1c", [128, 1], F32)
        w2c = sb("w2c", [128, 1], F32)
        aux = sb("aux", [128, 10], F32)
        auxu = sb("auxu", [128, 2], U32)
        bnst = sb("bnst", [128, 96], F32)
        mstat = sb("mstat", [128, 16], F32)
        rsq = sb("rsq", [128, 8], F32)
        rsi = sb("rsi", [128, 8], F32)
        nmb = sb("nmb", [128, 8], F32)

        with nc.Block() as block:
            def sec(engname, f):
                getattr(block, engname)(f)

            def phase_barrier():
                marks = [now(k) for k in ("qs", "qp", "pe", "act", "dve", "pool")]
                for engname in ("sync", "gpsimd", "vector", "scalar", "tensor"):
                    def bwait(eng, marks=marks):
                        wge(eng, *marks)
                    sec(engname, bwait)

            with contextlib.ExitStack() as p01:
                w01 = lambda name: p01.enter_context(nc.sbuf_tensor(_un(name), [128, 8 * D], BF16))
                wq_sb = w01("wq_sb"); wk_sb = w01("wk_sb")
                wv_sb = w01("wv_sb"); wo_sb = w01("wo_sb")

                with contextlib.ExitStack() as prep_st:
                    psb = lambda name, shape: prep_st.enter_context(nc.sbuf_tensor(_un(name), shape, F32))
                    lnrows = psb("lnrows", [8, 512])
                    hprow = psb("hprow", [1, 2 * NEXP])
                    b1rows = psb("b1rows", [32, LEVELS * 128])
                    b2row = psb("b2row", [1, LEVELS * D])
                    rbrow = psb("rbrow", [1, NEXP])
                    with contextlib.ExitStack() as pst:
                        pP = pst.enter_context(nc.psum_tensor(_un("pP"), [128, 512], F32))
                        pQ0 = pst.enter_context(nc.psum_tensor(_un("pQ0"), [128, 512], F32))
                        pQ1 = pst.enter_context(nc.psum_tensor(_un("pQ1"), [128, 512], F32))

                        def g_loads(g):
                            g.iota(iotacol[:], pattern=[[1, 1]], base=0, channel_multiplier=1,
                                   allow_small_or_imprecise_dtypes=True)
                            for wsb, wh in ((wq_sb, wq_h), (wk_sb, wk_h), (wv_sb, wv_h), (wo_sb, wo_h)):
                                inc(g.dma_start(out=wsb[:].rearrange("p (i j) -> p i j", j=D),
                                                in_=wh.rearrange("(i p) j -> p i j", p=128)), "qp", 16)
                            inc(g.dma_start(out=rw_sb[:].rearrange("p (i j) -> p i j", j=NEXP),
                                            in_=rw_h.rearrange("(i p) j -> p i j", p=128)), "qp", 16)
                            iv = g.iota(iota16[:], pattern=[[1, NEXP]], base=0, channel_multiplier=0,
                                        allow_small_or_imprecise_dtypes=True)
                            inc(iv, "pool")
                        sec("gpsimd", g_loads)
                        ev_qp_prep = now("qp")

                        def s_loads(s):
                            inc(s.dma_start(out=identb[:], in_=identb_h[:, :]), "qs", 16)
                            inc(s.dma_start(out=identf[:], in_=identf_h[:, :]), "qs", 16)
                            inc(s.dma_start(out=maskt[:], in_=maskt_h[:, :]), "qs", 16)
                            inc(s.dma_start(out=trius[:], in_=trius_h[:, :]), "qs", 16)
                            inc(s.dma_start(out=rankpre[:], in_=rankpre_h[:, :]), "qs", 16)
                            for i, hsrc in enumerate((ln1g_h, ln1b_h, ln2g_h, ln2b_h)):
                                inc(s.dma_start(out=lnrows[0:8, i * 128:(i + 1) * 128],
                                                in_=hsrc.rearrange("(q p) -> q p", p=128)), "qs", 16)
                            for l in range(LEVELS):
                                inc(s.dma_start(out=b1rows[0:32, l * 128:(l + 1) * 128],
                                                in_=cb1_h[l, :].rearrange("(q p) -> q p", p=128)), "qs", 16)
                            inc(s.dma_start(out=b2row[0:1, :],
                                            in_=cb2_h.rearrange("l j -> (l j)").rearrange("(a k) -> a k", a=1)),
                                "qs", 16)
                            inc(s.dma_start(out=rbrow[0:1, :],
                                            in_=rb_h.rearrange("(a k) -> a k", a=1)), "qs", 16)
                            inc(s.dma_start(out=hprow[0:1, :], in_=halfpar_h[:, :]), "qs", 16)
                            inc(s.dma_start(out=rankvec[:], in_=rankvec_h[:, :]), "qs", 16)
                        sec("sync", s_loads)
                        ev_qs_prep = now("qs")

                        def v_memsets(v):
                            wge(v, now("pool"))
                            v.tensor_scalar(out=oobcol[:], in0=iotacol[:], scalar1=1.0,
                                            op0=ALU.mult, scalar2=131072.0, op1=ALU.add)
                            v.tensor_scalar(out=zrowcol[:], in0=iotacol[:], scalar1=1.0,
                                            op0=ALU.mult, scalar2=float(16 * CAP), op1=ALU.add)
                            v.memset(epsv[:], EPS)
                            v.memset(onesf[:], 1.0)
                            v.memset(onesbc[:], 1.0)
                            v.memset(onesgc[:], 1.0)
                            v.memset(zerosb[:], 0.0)
                            iv = v.memset(base[:], 0.0)
                            inc(iv, "dve")
                        sec("vector", v_memsets)
                        ev_memset = now("dve")

                        def t_prep(t):
                            wge(t, ev_qs_prep)
                            for i in range(4):
                                t.transpose(pP[:, i * 8:(i + 1) * 8],
                                            lnrows[0:8, i * 128:(i + 1) * 128], identf[0:8, 0:8])
                            for l in range(LEVELS):
                                iv = t.transpose(pP[:, 32 + l * 32:64 + l * 32],
                                                 b1rows[0:32, l * 128:(l + 1) * 128], identf[0:32, 0:32])
                            inc(iv, "pe")
                        sec("tensor", t_prep)
                        ev_pe_prep = now("pe")

                        def v_prep(v):
                            wge(v, ev_pe_prep)
                            iv = v.tensor_copy(consts[:], pP[:, 0:128])
                            inc(iv, "dve")
                        sec("vector", v_prep)
                        ev_consts = now("dve")

                        bc_ev = {}
                        dve_base_bc = cnt["dve"]

                        def t_bc(t):
                            wge(t, ev_consts)
                            for i in range(9):
                                bank = pQ0 if i % 2 == 0 else pQ1
                                if i >= 2:
                                    wge(t, ("dve", dve_base_bc + i - 1))
                                if i < 6:
                                    iv = t.matmul(bank[:], onesf[:], b2row[0:1, i * 512:(i + 1) * 512],
                                                  start=True, stop=True)
                                elif i == 6:
                                    iv = t.matmul(bank[:, 0:NEXP], onesf[:], rbrow[0:1, :],
                                                  start=True, stop=True)
                                else:
                                    iv = t.matmul(bank[:, 0:NEXP], onesf[:],
                                                  hprow[0:1, (i - 7) * NEXP:(i - 6) * NEXP],
                                                  start=True, stop=True)
                                bc_ev[("m", i)] = inc(iv, "pe")
                        sec("tensor", t_bc)

                        def v_bc(v):
                            for i in range(9):
                                bank = pQ0 if i % 2 == 0 else pQ1
                                wge(v, bc_ev[("m", i)])
                                if i < 6:
                                    iv = v.tensor_copy(b2bc[:, i * 512:(i + 1) * 512], bank[:, 0:512])
                                elif i == 6:
                                    iv = v.tensor_copy(rbbc[:], bank[:, 0:NEXP])
                                elif i == 7:
                                    iv = v.tensor_copy(halfbc[:], bank[:, 0:NEXP])
                                else:
                                    iv = v.tensor_copy(parbc[:], bank[:, 0:NEXP])
                                inc(iv, "dve")
                        sec("vector", v_bc)
                        assert cnt["dve"] == dve_base_bc + 9
                        prep_end = {k: now(k) for k in sem}

                phase_barrier()
                with contextlib.ExitStack() as p1:
                    a = lambda name, shape, dt: p1.enter_context(nc.sbuf_tensor(_un(name), shape, dt))
                    xt = a("xt", [128, NSUB * D], F32)
                    z1 = [a("z1", [128, NSUB * D], F32) for _ in range(2)]
                    z1n = a("z1n", [128, NSUB * D], BF16)
                    h0 = a("h0", [128, NSUB * D], BF16)
                    qf = a("qf", [128, NSUB * D], BF16)
                    kf = a("kf", [128, NSUB * D], BF16)
                    vt = a("vt", [128, NSUB * D], BF16)
                    kt = a("kt", [128, NSUB * D], BF16)
                    yf = a("yf", [128, NSUB * D], BF16)
                    xa = a("xa", [128, NSUB * D], F32)
                    hno = yf
                    mts = a("mts", [128, 512], BF16)
                    mtsf = a("mtsf", [128, 512], F32)
                    scs0 = a("scs0", [128, 128], BF16)
                    scs1 = a("scs1", [128, 128], BF16)
                    with contextlib.ExitStack() as pst:
                        ps = lambda name, shape, dt=F32: pst.enter_context(nc.psum_tensor(_un(name), shape, dt))
                        pDM = ps("pDM", [128, 512])
                        pT0 = ps("pT0", [128, 1024], BF16)
                        pT1 = ps("pT1", [128, 1024], BF16)
                        pA = ps("pA", [128, 512])
                        pB = ps("pB", [128, 512])
                        pS0 = ps("pS0", [128, 128])
                        pS1 = ps("pS1", [128, 128])

                        def ln_norm(src, dst, ev_ready, war_ev=None, act_out=False):
                            evd = {}

                            def v1(v):
                                wge(v, ev_ready)
                                for s in range(NSUB):
                                    c0 = s * D
                                    v.bn_stats(bnst[:, s * 12:s * 12 + 6], src[:, c0:c0 + 512])
                                    ev = inc(v.bn_stats(bnst[:, s * 12 + 6:s * 12 + 12],
                                                        src[:, c0 + 512:c0 + D]), "dve")
                                    wge(v, ev)
                                    ev = inc(v.bn_aggr(mstat[:, 2 * s:2 * s + 2],
                                                       bnst[:, s * 12:s * 12 + 12]), "dve")
                                    evd[("agg", s)] = ev
                            sec("vector", v1)

                            def s1(sc):
                                for s in range(NSUB):
                                    wge(sc, evd[("agg", s)])
                                    iv = sc.activation(out=rsq[:, s:s + 1], in_=mstat[:, 2 * s + 1:2 * s + 2],
                                                       func=AF.Sqrt, bias=epsv[:])
                                    evd[("sq", s)] = inc(iv, "act")
                            sec("scalar", s1)

                            if act_out:
                                def v2s(v):
                                    for s in range(NSUB):
                                        wge(v, evd[("sq", s)])
                                        ev = inc(v.reciprocal(rsi[:, s:s + 1], rsq[:, s:s + 1]), "dve")
                                        wge(v, ev)
                                        iv = v.tensor_scalar(out=nmb[:, s:s + 1],
                                                             in0=mstat[:, 2 * s:2 * s + 1],
                                                             scalar1=rsi[:, s:s + 1], op0=ALU.mult,
                                                             scalar2=-1.0, op1=ALU.mult)
                                        evd[("nm", s)] = inc(iv, "dve")
                                sec("vector", v2s)

                                def s2(sc):
                                    if war_ev is not None:
                                        wge(sc, war_ev)
                                    for s in range(NSUB):
                                        wge(sc, evd[("nm", s)])
                                        c0 = s * D
                                        for half in range(2):
                                            iv = sc.activation(
                                                out=dst[:, c0 + half * 512:c0 + half * 512 + 512],
                                                in_=src[:, c0 + half * 512:c0 + half * 512 + 512],
                                                func=AF.Identity, scale=rsi[:, s:s + 1],
                                                bias=nmb[:, s:s + 1])
                                            inc(iv, "act")
                                sec("scalar", s2)
                                return now("act")

                            def v2(v):
                                if war_ev is not None:
                                    wge(v, war_ev)
                                for s in range(NSUB):
                                    wge(v, evd[("sq", s)])
                                    ev = inc(v.reciprocal(rsi[:, s:s + 1], rsq[:, s:s + 1]), "dve")
                                    wge(v, ev)
                                    c0 = s * D
                                    iv = v.tensor_scalar(out=dst[:, c0:c0 + D], in0=src[:, c0:c0 + D],
                                                         scalar1=mstat[:, 2 * s:2 * s + 1], op0=ALU.subtract,
                                                         scalar2=rsi[:, s:s + 1], op1=ALU.mult)
                                    inc(iv, "dve")
                            sec("vector", v2)
                            return now("dve")

                        def v_init(v):
                            v.memset(mts[:], 0.0)
                            inc(v.memset(mtsf[:], 0.0), "dve")
                        sec("vector", v_init)
                        ev_mts0 = now("dve")

                        ev_xt_free = [None]
                        ev_z1n_war = [None]
                        ev_z1n_done = [None]
                        ev_spill_xa = [None] * NT
                        ev_spill_h = [None] * NT
                        attn_gate = {}
                        prev_wo_tail = None

                        def emit_front(tau):
                            par = tau % 2
                            hold = {}

                            def s_load(sy, tau=tau, free=ev_xt_free[0]):
                                if tau == 0:
                                    wge(sy, prep_end["qs"])
                                wge(sy, free)
                                wge(sy, ("qs", 1))
                                iv = sy.dma_start(
                                    out=xt[:].rearrange("p (s j) -> p s j", j=D),
                                    in_=x_h[tau * NTOK:(tau + 1) * NTOK, :].rearrange("(s p) j -> p s j", p=128))
                                hold["x"] = inc(iv, "qs", 16)
                            sec("sync", s_load)

                            ev_z1 = ln_norm(xt, z1[par], hold["x"], act_out=True)
                            ev_xt_free[0] = ev_z1
                            ln_norm(z1[par], z1n, ev_z1, war_ev=ev_z1n_war[0], act_out=True)
                            ev_z1n_done[0] = now("act")

                        emit_front(0)
                        for tau in range(NT):
                            par = tau % 2
                            act_base = cnt["act"]
                            tr_ev = {}

                            def t_tr(t, tau=tau, ev_z1n=ev_z1n_done[0], act_base=act_base):
                                wge(t, ev_z1n)
                                for ic in range(8):
                                    bank = pT0 if ic % 2 == 0 else pT1
                                    if ic >= 2:
                                        wge(t, ("act", act_base + ic - 1))
                                    for s in range(NSUB):
                                        iv = t.transpose(bank[:, s * 128:(s + 1) * 128],
                                                         z1n[:, s * D + ic * 128:s * D + ic * 128 + 128],
                                                         identb[:])
                                    tr_ev[("t", ic)] = inc(iv, "pe")
                            sec("tensor", t_tr)
                            ev_z1n_war[0] = now("pe")

                            def s_tr(sc, tau=tau):
                                for ic in range(8):
                                    bank = pT0 if ic % 2 == 0 else pT1
                                    wge(sc, tr_ev[("t", ic)])
                                    iv = sc.activation(out=h0[:, ic * 512:(ic + 1) * 512],
                                                       in_=bank[:, 0:512], func=AF.Identity,
                                                       scale=consts[:, ic:ic + 1],
                                                       bias=consts[:, 8 + ic:9 + ic])
                                    inc(iv, "act")
                            sec("scalar", s_tr)
                            assert cnt["act"] == act_base + 8
                            ev_h0 = now("act")

                            act_base = cnt["act"]
                            qkv_ev = {}

                            def t_qkv(t, tau=tau, ev_h0=ev_h0, act_base=act_base,
                                      wo_tail=prev_wo_tail):
                                wge(t, ev_h0)
                                wge(t, wo_tail)
                                if tau == 0:
                                    wge(t, prep_end["qp"])
                                pb = 0
                                for wsb in (wq_sb, wk_sb):
                                    for oc in range(8):
                                        bank = pA if pb % 2 == 0 else pB
                                        if pb >= 2:
                                            wge(t, ("act", act_base + pb - 1))
                                        for ic in range(8):
                                            iv = t.matmul(bank[:],
                                                          wsb[:, ic * D + oc * 128:ic * D + oc * 128 + 128],
                                                          h0[:, ic * 512:(ic + 1) * 512],
                                                          start=(ic == 0), stop=(ic == 7))
                                        qkv_ev[("m", pb)] = inc(iv, "pe")
                                        pb += 1
                                for s in range(NSUB):
                                    for half in range(2):
                                        bank = pA if pb % 2 == 0 else pB
                                        wge(t, ("act", act_base + pb - 1))
                                        for ic in range(8):
                                            iv = t.matmul(bank[:],
                                                          h0[:, ic * 512 + s * 128:ic * 512 + s * 128 + 128],
                                                          wv_sb[:, ic * D + half * 512:ic * D + half * 512 + 512],
                                                          start=(ic == 0), stop=(ic == 7))
                                        qkv_ev[("m", pb)] = inc(iv, "pe")
                                        pb += 1
                            sec("tensor", t_qkv)

                            def s_qkv(sc, tau=tau):
                                pb = 0
                                for dest in (qf, kf):
                                    for oc in range(8):
                                        bank = pA if pb % 2 == 0 else pB
                                        wge(sc, qkv_ev[("m", pb)])
                                        iv = sc.copy(out=dest[:, oc * 512:(oc + 1) * 512], in_=bank[:])
                                        inc(iv, "act")
                                        pb += 1
                                for s in range(NSUB):
                                    for half in range(2):
                                        bank = pA if pb % 2 == 0 else pB
                                        wge(sc, qkv_ev[("m", pb)])
                                        iv = sc.copy(out=vt[:, s * D + half * 512:s * D + half * 512 + 512],
                                                     in_=bank[:])
                                        inc(iv, "act")
                                        pb += 1
                            sec("scalar", s_qkv)
                            assert cnt["act"] == act_base + 24
                            ev_qkv = now("act")

                            dve_base = cnt["dve"]
                            kt_ev = {}

                            def t_kt(t, tau=tau, ev_qkv=ev_qkv, dve_base=dve_base):
                                wge(t, ev_qkv)
                                for s in range(NSUB):
                                    bank = pT0 if s % 2 == 0 else pT1
                                    if s >= 2:
                                        wge(t, ("dve", dve_base + s - 1))
                                    for ic in range(8):
                                        iv = t.transpose(bank[:, ic * 128:(ic + 1) * 128],
                                                         kf[:, ic * 512 + s * 128:ic * 512 + s * 128 + 128],
                                                         identb[:])
                                    kt_ev[("t", s)] = inc(iv, "pe")
                            sec("tensor", t_kt)

                            def v_kt(v, tau=tau):
                                for s in range(NSUB):
                                    bank = pT0 if s % 2 == 0 else pT1
                                    wge(v, kt_ev[("t", s)])
                                    iv = v.tensor_copy(kt[:, s * D:(s + 1) * D], bank[:])
                                    inc(iv, "dve")
                            sec("vector", v_kt)
                            assert cnt["dve"] == dve_base + 4
                            ev_kt = now("dve")

                            for s in range(NSUB):
                                pe_base = cnt["pe"]
                                dve_base = cnt["dve"]
                                sc_ev = [("pe", pe_base + i + 1) for i in range(HEADS)]
                                msk_ev = [("dve", dve_base + i + 1) for i in range(HEADS)]

                                def t_at(t, tau=tau, s=s, ev_kt=ev_kt, sc_ev=sc_ev, msk_ev=msk_ev,
                                         gate=dict(attn_gate)):
                                    wge(t, ev_kt)
                                    if tau == 0 and s == 0:
                                        wge(t, prep_end["dve"], ev_mts0)
                                    else:
                                        wge(t, gate["mtcopy"], gate["ytevac"])
                                    for h in range(HEADS):
                                        p0 = (h % 2) * 64
                                        oc = h // 2
                                        sbank = pS0 if h % 2 == 0 else pS1
                                        if h >= 2:
                                            wge(t, msk_ev[h - 2])
                                        iv = t.matmul(
                                            sbank[:],
                                            kf[p0:p0 + 64, oc * 512 + s * 128:oc * 512 + s * 128 + 128],
                                            qf[p0:p0 + 64, oc * 512 + s * 128:oc * 512 + s * 128 + 128],
                                            start=True, stop=True)
                                        inc(iv, "pe")
                                        t.matmul(pDM[p0:p0 + 64, oc * 64:oc * 64 + 64],
                                                 kt[:, s * D + h * 64:s * D + h * 64 + 64],
                                                 vt[:, s * D + h * 64:s * D + h * 64 + 64],
                                                 start=(h < 2), stop=(h >= HEADS - 2),
                                                 skip_group_check=True)
                                        ybank = pA if h < 8 else pB
                                        ycol = (oc % 4) * 128
                                        wge(t, msk_ev[h])
                                        t.matmul(ybank[p0:p0 + 64, ycol:ycol + 128],
                                                 mts[p0:p0 + 64, oc * 64:oc * 64 + 64],
                                                 qf[p0:p0 + 64, oc * 512 + s * 128:oc * 512 + s * 128 + 128],
                                                 start=True, stop=False, skip_group_check=True)
                                        scsrc = scs0 if h % 2 == 0 else scs1
                                        iv = t.matmul(ybank[p0:p0 + 64, ycol:ycol + 128],
                                                      vt[:, s * D + h * 64:s * D + h * 64 + 64],
                                                      scsrc[:],
                                                      start=False, stop=True, skip_group_check=True)
                                    inc(iv, "pe")
                                sec("tensor", t_at)
                                assert cnt["pe"] == pe_base + HEADS + 1
                                pe_all = ("pe", pe_base + HEADS + 1)

                                def v_at(v, tau=tau, s=s, sc_ev=sc_ev, pe_all=pe_all):
                                    if s == 0 and tau > 0:
                                        wge(v, ev_spill_h[tau - 1])
                                    for h in range(HEADS):
                                        sbank = pS0 if h % 2 == 0 else pS1
                                        scdst = scs0 if h % 2 == 0 else scs1
                                        wge(v, sc_ev[h])
                                        iv = v.tensor_tensor(out=scdst[:], in0=sbank[:], in1=maskt[:],
                                                             op=ALU.mult)
                                        inc(iv, "dve")
                                    wge(v, pe_all)
                                    iv = v.tensor_copy(
                                        yf[:].rearrange("p (oc t) -> p oc t", t=512)[:, 0:4, s * 128:s * 128 + 128],
                                        pA[:].rearrange("p (oc t) -> p oc t", t=128))
                                    inc(iv, "dve")
                                    iv = v.tensor_copy(
                                        yf[:].rearrange("p (oc t) -> p oc t", t=512)[:, 4:8, s * 128:s * 128 + 128],
                                        pB[:].rearrange("p (oc t) -> p oc t", t=128))
                                    attn_gate["ytevac"] = inc(iv, "dve")
                                    iv = v.tensor_tensor(out=mtsf[:], in0=mtsf[:], in1=pDM[:],
                                                         op=ALU.add)
                                    ev = inc(iv, "dve")
                                    wge(v, ev)
                                    iv = v.tensor_copy(mts[:], mtsf[:])
                                    attn_gate["mtcopy"] = inc(iv, "dve")
                                sec("vector", v_at)
                                assert cnt["dve"] == dve_base + HEADS + 4

                            ev_attn = now("dve")

                            if tau + 1 < NT:
                                emit_front(tau + 1)

                            dve_base = cnt["dve"]
                            wo_ev = {}

                            def t_wo(t, tau=tau, ev_attn=ev_attn, dve_base=dve_base):
                                wge(t, ev_attn)
                                pb = 0
                                for s in range(NSUB):
                                    for half in range(2):
                                        bank = pA if pb % 2 == 0 else pB
                                        if pb >= 2:
                                            wge(t, ("dve", dve_base + pb - 1))
                                        for oc in range(8):
                                            iv = t.matmul(bank[:],
                                                          yf[:, oc * 512 + s * 128:oc * 512 + s * 128 + 128],
                                                          wo_sb[:, oc * D + half * 512:oc * D + half * 512 + 512],
                                                          start=(oc == 0), stop=(oc == 7))
                                        wo_ev[("m", pb)] = inc(iv, "pe")
                                        pb += 1
                            sec("tensor", t_wo)

                            def v_wo(v, tau=tau):
                                pb = 0
                                for s in range(NSUB):
                                    for half in range(2):
                                        bank = pA if pb % 2 == 0 else pB
                                        wge(v, wo_ev[("m", pb)])
                                        if pb == 0 and tau > 0:
                                            wge(v, ev_spill_xa[tau - 1])
                                        c0 = s * D + half * 512
                                        iv = v.tensor_tensor(out=xa[:, c0:c0 + 512], in0=bank[:],
                                                             in1=z1[par][:, c0:c0 + 512], op=ALU.add)
                                        inc(iv, "dve")
                                        pb += 1
                            sec("vector", v_wo)
                            assert cnt["dve"] == dve_base + 8
                            ev_wo = now("dve")
                            prev_wo_tail = ev_wo

                            def s_spill_xa(s, tau=tau, ev_wo=ev_wo):
                                wge(s, ev_wo)
                                iv = s.dma_start(
                                    out=sp_xa[tau * NTOK:(tau + 1) * NTOK, :].rearrange("(s p) j -> p s j", p=128),
                                    in_=xa[:].rearrange("p (s j) -> p s j", j=D))
                                ev_spill_xa[tau] = inc(iv, "qs", 16)
                            sec("sync", s_spill_xa)

                            ev_hno = ln_norm(xa, hno, ev_wo,
                                             war_ev=ev_spill_h[tau - 1] if tau > 0 else None)

                            def s_spill_h(s, tau=tau, ev_hno=ev_hno):
                                wge(s, ev_hno)
                                iv = s.dma_start(
                                    out=sp_h[tau * NTOK:(tau + 1) * NTOK, :].rearrange("(s p) j -> p s j", p=128),
                                    in_=hno[:].rearrange("p (s j) -> p s j", j=D))
                                ev_spill_h[tau] = inc(iv, "qs", 16)
                            sec("sync", s_spill_h)

                        p1_end = {k: now(k) for k in sem}

            if phases >= 2:
              phase_barrier()
              with contextlib.ExitStack() as p2:
                a = lambda name, shape, dt: p2.enter_context(nc.sbuf_tensor(_un(name), shape, dt))
                w1_sb = a("w1_sb", [128, 8 * HID], BF16)
                w2_sb = a("w2_sb", [128, 32 * D], BF16)
                ttm = a("ttm", [128, NSUB * D], BF16)
                infm = a("infm", [128, NSUB * D], BF16)
                hid = a("hid", [128, 32 * 512], BF16)
                hf = a("hf", [128, D], F32)
                with contextlib.ExitStack() as pst:
                    ps = lambda name, shape, dt=F32: pst.enter_context(nc.psum_tensor(_un(name), shape, dt))
                    pT0 = ps("pT0", [128, 512], BF16)
                    pT1 = ps("pT1", [128, 512], BF16)
                    pA = ps("pA", [128, 512])
                    pB = ps("pB", [128, 512])

                    lvl_src = [sp_h, sp_hB, sp_hA]
                    lvl_dst = [sp_hB, sp_hA, None]
                    lvl_spill = [[None] * NT for _ in range(LEVELS)]
                    ev_pe_lvl = p1_end["pe"]
                    ev_up_lvl = p1_end["pe"]

                    for l in range(LEVELS):
                        def g_w(g, l=l, ev_pe_lvl=ev_pe_lvl, ev_up_lvl=ev_up_lvl):
                            wge(g, ev_up_lvl)
                            iv = g.dma_start(out=w1_sb[:].rearrange("p (i j) -> p i j", j=HID),
                                             in_=cw1_h[l].rearrange("(i p) j -> p i j", p=128))
                            inc(iv, "qp", 16)
                            wge(g, ev_pe_lvl)
                            iv = g.dma_start(out=w2_sb[:].rearrange("p (i j) -> p i j", j=D),
                                             in_=cw2_h[l].rearrange("(i p) j -> p i j", p=128))
                            inc(iv, "qp", 16)
                        sec("gpsimd", g_w)
                        ev_w = now("qp")

                        ev_ttm_free = None
                        prev_dn_tail = None
                        for tau in range(NT):
                            hold = {}

                            def s_ld(s, l=l, tau=tau, ev_ttm_free=ev_ttm_free):
                                if l == 0:
                                    wge(s, ev_spill_h[tau])
                                else:
                                    wge(s, lvl_spill[l - 1][tau])
                                wge(s, ev_ttm_free)
                                iv = s.dma_start(
                                    out=ttm[:].rearrange("p (s j) -> p s j", j=D),
                                    in_=lvl_src[l][tau * NTOK:(tau + 1) * NTOK, :].rearrange(
                                        "(s p) j -> p s j", p=128))
                                hold["ttm"] = inc(iv, "qs", 16)
                            sec("sync", s_ld)
                            ev_ttm = hold["ttm"]

                            act_base = cnt["act"]
                            tr_ev = {}

                            def t_tr(t, l=l, tau=tau, ev_ttm=ev_ttm, act_base=act_base):
                                wge(t, ev_ttm)
                                for ic in range(8):
                                    bank = pT0 if ic % 2 == 0 else pT1
                                    if ic >= 2:
                                        wge(t, ("act", act_base + ic - 1))
                                    for s in range(NSUB):
                                        iv = t.transpose(bank[:, s * 128:(s + 1) * 128],
                                                         ttm[:, s * D + ic * 128:s * D + ic * 128 + 128],
                                                         identb[:])
                                    tr_ev[("t", ic)] = inc(iv, "pe")
                            sec("tensor", t_tr)
                            ev_ttm_free = now("pe")

                            def s_tr(sc, l=l, tau=tau):
                                for ic in range(8):
                                    bank = pT0 if ic % 2 == 0 else pT1
                                    wge(sc, tr_ev[("t", ic)])
                                    if l == 0:
                                        iv = sc.activation(out=infm[:, ic * 512:(ic + 1) * 512],
                                                           in_=bank[:], func=AF.Identity,
                                                           scale=consts[:, 16 + ic:17 + ic],
                                                           bias=consts[:, 24 + ic:25 + ic])
                                    else:
                                        iv = sc.copy(out=infm[:, ic * 512:(ic + 1) * 512], in_=bank[:])
                                    inc(iv, "act")
                            sec("scalar", s_tr)
                            assert cnt["act"] == act_base + 8
                            ev_infm = now("act")

                            act_base = cnt["act"]
                            up_ev = {}

                            def t_up(t, l=l, tau=tau, ev_infm=ev_infm, ev_w=ev_w, act_base=act_base,
                                     dn_tail=prev_dn_tail):
                                wge(t, ev_infm, ev_w)
                                wge(t, dn_tail)
                                for oc in range(32):
                                    bank = pA if oc % 2 == 0 else pB
                                    if oc >= 2:
                                        wge(t, ("act", act_base + oc - 1))
                                    for ic in range(8):
                                        iv = t.matmul(bank[:],
                                                      w1_sb[:, ic * HID + oc * 128:ic * HID + oc * 128 + 128],
                                                      infm[:, ic * 512:(ic + 1) * 512],
                                                      start=(ic == 0), stop=(ic == 7))
                                    up_ev[("m", oc)] = inc(iv, "pe")
                            sec("tensor", t_up)

                            def s_up(sc, l=l, tau=tau):
                                for oc in range(32):
                                    bank = pA if oc % 2 == 0 else pB
                                    wge(sc, up_ev[("m", oc)])
                                    iv = sc.activation(out=hid[:, oc * 512:(oc + 1) * 512], in_=bank[:],
                                                       func=AF.Gelu_apprx_tanh,
                                                       bias=consts[:, 32 + l * 32 + oc:33 + l * 32 + oc])
                                    inc(iv, "act")
                            sec("scalar", s_up)
                            assert cnt["act"] == act_base + 32
                            ev_hid = now("act")
                            if tau == NT - 1:
                                ev_up_lvl_next = now("pe")

                            dve_base = cnt["dve"]
                            dn_ev = {}
                            hf_sp = {}
                            for s in range(NSUB):
                                def t_dn(t, l=l, tau=tau, s=s, ev_hid=ev_hid, dve_base=dve_base):
                                    if s == 0:
                                        wge(t, ev_hid)
                                    for half in range(2):
                                        pb = s * 2 + half
                                        bank = pA if pb % 2 == 0 else pB
                                        if pb >= 2:
                                            wge(t, ("dve", dve_base + pb - 1))
                                        for oc in range(32):
                                            iv = t.matmul(bank[:],
                                                          hid[:, oc * 512 + s * 128:oc * 512 + s * 128 + 128],
                                                          w2_sb[:, oc * D + half * 512:oc * D + half * 512 + 512],
                                                          start=(oc == 0), stop=(oc == 31))
                                        dn_ev[("m", pb)] = inc(iv, "pe")
                                sec("tensor", t_dn)

                                def v_dn(v, l=l, tau=tau, s=s):
                                    for half in range(2):
                                        pb = s * 2 + half
                                        bank = pA if pb % 2 == 0 else pB
                                        wge(v, dn_ev[("m", pb)])
                                        if l < 2:
                                            iv = v.tensor_tensor(
                                                out=ttm[:, s * D + half * 512:s * D + half * 512 + 512],
                                                in0=bank[:],
                                                in1=b2bc[:, l * D + half * 512:l * D + half * 512 + 512],
                                                op=ALU.add)
                                        else:
                                            if half == 0 and s >= 1:
                                                wge(v, hf_sp[s - 1])
                                            iv = v.tensor_tensor(
                                                out=hf[:, half * 512:half * 512 + 512],
                                                in0=bank[:],
                                                in1=b2bc[:, 2 * D + half * 512:2 * D + half * 512 + 512],
                                                op=ALU.add)
                                        inc(iv, "dve")
                                sec("vector", v_dn)

                                if l == 2:
                                    def s_sp3(sy, tau=tau, s=s):
                                        wge(sy, now("dve"))
                                        iv = sy.dma_start(
                                            out=sp_h3[tau * NTOK + s * 128:tau * NTOK + (s + 1) * 128, :],
                                            in_=hf[:])
                                        hf_sp[s] = inc(iv, "qs", 16)
                                    sec("sync", s_sp3)
                            assert cnt["dve"] == dve_base + 8
                            prev_dn_tail = now("dve")

                            if l < 2:
                                ev_out = now("dve")

                                def s_sp(s, l=l, tau=tau, ev_out=ev_out):
                                    wge(s, ev_out)
                                    iv = s.dma_start(
                                        out=lvl_dst[l][tau * NTOK:(tau + 1) * NTOK, :].rearrange(
                                            "(s p) j -> p s j", p=128),
                                        in_=ttm[:].rearrange("p (s j) -> p s j", j=D))
                                    hold["sp"] = inc(iv, "qs", 16)
                                sec("sync", s_sp)
                                lvl_spill[l][tau] = hold["sp"]
                                ev_ttm_free = hold["sp"]
                            else:
                                lvl_spill[l][tau] = now("qs")
                        ev_pe_lvl = now("pe")
                        ev_up_lvl = ev_up_lvl_next

                    p2_end = {k: now(k) for k in sem}

            if phases >= 2.5:
              phase_barrier()
              with contextlib.ExitStack() as p2b:
                a = lambda name, shape, dt: p2b.enter_context(nc.sbuf_tensor(_un(name), shape, dt))
                h3t = [a("h3t", [128, NSUB * D], F32) for _ in range(2)]
                xat = [a("xat", [128, NSUB * D], F32) for _ in range(2)]
                xt2 = [a("xt2", [128, NSUB * D], F32) for _ in range(2)]
                oatt = a("oatt", [128, NSUB * D], F32)
                zpst = a("zpst", [128, NSUB * D], F32)
                x2tt = a("x2tt", [128, NSUB * D], F32)
                z2tt = a("z2tt", [128, NSUB * D], F32)
                z2nb = a("z2nb", [128, NSUB * D], BF16)
                z2fm = a("z2fm", [128, NSUB * D], BF16)
                logit = a("logit", [128, NEXP], F32)
                expz = a("expz", [128, NEXP], F32)
                probs = a("probs", [128, NEXP], F32)
                top8v = a("top8v", [128, 8], F32)
                top8i = a("top8i", [128, 8], U32)
                idxf = a("idxf", [128, 2], F32)
                sm1 = a("sm1", [128, 1], F32)
                sm2 = a("sm2", [128, 1], F32)
                with contextlib.ExitStack() as pst:
                    ps = lambda name, shape, dt=F32: pst.enter_context(nc.psum_tensor(_un(name), shape, dt))
                    pT0 = ps("pT0", [128, 512], BF16)
                    pT1 = ps("pT1", [128, 512], BF16)
                    pR = ps("pR", [128, NEXP])
                    pCNT = ps("pCNT", [1, NEXP])
                    pBCB = ps("pBCB", [128, NEXP])

                    def ln0_sub(src, dst, ev_ready, war_ev=None, bf_dst=None, bf_war=None,
                                slot=0):
                        b0 = slot * 12
                        def v1(v):
                            wge(v, ev_ready)
                            v.bn_stats(bnst[:, b0:b0 + 6], src[:, 0:512])
                            ev = inc(v.bn_stats(bnst[:, b0 + 6:b0 + 12], src[:, 512:D]), "dve")
                            wge(v, ev)
                            inc(v.bn_aggr(mstat[:, 2 * slot:2 * slot + 2], bnst[:, b0:b0 + 12]), "dve")
                        sec("vector", v1)
                        eva = now("dve")

                        def s1(sc):
                            wge(sc, eva)
                            inc(sc.activation(out=rsq[:, slot:slot + 1],
                                              in_=mstat[:, 2 * slot + 1:2 * slot + 2],
                                              func=AF.Sqrt, bias=epsv[:]), "act")
                        sec("scalar", s1)
                        evs_ = now("act")

                        def v2(v):
                            wge(v, evs_, war_ev)
                            ev = inc(v.reciprocal(rsi[:, slot:slot + 1], rsq[:, slot:slot + 1]), "dve")
                            wge(v, ev)
                            iv = v.tensor_scalar(out=dst[:, 0:D], in0=src[:, 0:D],
                                                 scalar1=mstat[:, 2 * slot:2 * slot + 1], op0=ALU.subtract,
                                                 scalar2=rsi[:, slot:slot + 1], op1=ALU.mult)
                            ev = inc(iv, "dve")
                            if bf_dst is not None:
                                wge(v, ev, *(bf_war or ()))
                                inc(v.tensor_copy(bf_dst, dst[:, 0:D]), "dve")
                        sec("vector", v2)
                        return now("dve")

                    ev_ld_free = [None, None]
                    ev_z2nb_war = None
                    ev_z2tm_war = None
                    ev_spfm_prev = None
                    ev_pool_tail = None
                    ev_x2sp = None
                    ev_z2sp = None
                    rt_prev = {}
                    for tau in range(NT):
                        par2 = tau % 2
                        h3t_, xat_, xt_ = h3t[par2], xat[par2], xt2[par2]
                        r0 = tau * NTOK
                        hold = {}

                        def s_ld(sy, r0=r0, h3t_=h3t_, xat_=xat_, xt_=xt_,
                                 free=ev_ld_free[par2], first=(tau == 0)):
                            if first:
                                wge(sy, p2_end["qs"])
                            wge(sy, free)
                            inc(sy.dma_start(out=h3t_[:].rearrange("p (s j) -> p s j", j=D),
                                             in_=sp_h3[r0:r0 + NTOK, :].rearrange("(s p) j -> p s j", p=128)), "qs", 16)
                            inc(sy.dma_start(out=xat_[:].rearrange("p (s j) -> p s j", j=D),
                                             in_=sp_xa[r0:r0 + NTOK, :].rearrange("(s p) j -> p s j", p=128)), "qs", 16)
                            hold["ld"] = inc(sy.dma_start(out=xt_[:].rearrange("p (s j) -> p s j", j=D),
                                             in_=x_h[r0:r0 + NTOK, :].rearrange("(s p) j -> p s j", p=128)), "qs", 16)
                        sec("sync", s_ld)

                        ag1 = {}

                        def v_st1(v, h3t_=h3t_, xat_=xat_):
                            wge(v, hold["ld"])
                            for sx in range(NSUB):
                                c0 = sx * D
                                ev = inc(v.tensor_tensor(out=oatt[:, c0:c0 + D], in0=h3t_[:, c0:c0 + D],
                                                         in1=xat_[:, c0:c0 + D], op=ALU.add), "dve")
                            wge(v, ev)
                            for sx in range(NSUB):
                                c0 = sx * D
                                v.bn_stats(bnst[:, sx * 12:sx * 12 + 6], oatt[:, c0:c0 + 512])
                                ev = inc(v.bn_stats(bnst[:, sx * 12 + 6:sx * 12 + 12],
                                                    oatt[:, c0 + 512:c0 + D]), "dve")
                                wge(v, ev)
                                ag1[sx] = inc(v.bn_aggr(mstat[:, 2 * sx:2 * sx + 2],
                                                        bnst[:, sx * 12:sx * 12 + 12]), "dve")
                        sec("vector", v_st1)

                        def s_sq1(sc):
                            for sx in range(NSUB):
                                wge(sc, ag1[sx])
                                ag1[("q", sx)] = inc(sc.activation(out=rsq[:, sx:sx + 1],
                                                                   in_=mstat[:, 2 * sx + 1:2 * sx + 2],
                                                                   func=AF.Sqrt, bias=epsv[:]), "act")
                        sec("scalar", s_sq1)

                        def v_n1(v):
                            for sx in range(NSUB):
                                wge(v, ag1[("q", sx)])
                                ev = inc(v.reciprocal(rsi[:, sx:sx + 1], rsq[:, sx:sx + 1]), "dve")
                                wge(v, ev)
                                c0 = sx * D
                                inc(v.tensor_scalar(out=zpst[:, c0:c0 + D], in0=oatt[:, c0:c0 + D],
                                                    scalar1=mstat[:, 2 * sx:2 * sx + 1], op0=ALU.subtract,
                                                    scalar2=rsi[:, sx:sx + 1], op1=ALU.mult), "dve")
                        sec("vector", v_n1)
                        ev_zp = now("dve")

                        def p_x2(g, xt_=xt_, ev_zp=ev_zp, ev_x2sp=ev_x2sp):
                            wge(g, ev_zp, ev_x2sp)
                            for sx in range(NSUB):
                                c0 = sx * D
                                inc(g.tensor_add(out=x2tt[:, c0:c0 + D], in0=zpst[:, c0:c0 + D],
                                                 in1=xt_[:, c0:c0 + D]), "pool")
                        sec("gpsimd", p_x2)
                        ev_x2 = now("pool")
                        ev_pool_tail = ev_x2

                        def s_spx2(sy, r0=r0, ev_x2=ev_x2):
                            wge(sy, ev_x2)
                            hold["x2sp"] = inc(sy.dma_start(
                                out=sp_x2[r0:r0 + NTOK, :].rearrange("(s p) j -> p s j", p=128),
                                in_=x2tt[:].rearrange("p (s j) -> p s j", j=D)), "qs", 16)
                        sec("sync", s_spx2)
                        ev_x2sp = hold["x2sp"]

                        ag2 = {}

                        def v_st2(v, ev_x2=ev_x2):
                            wge(v, ev_x2)
                            for sx in range(NSUB):
                                c0 = sx * D
                                v.bn_stats(bnst[:, 48 + sx * 12:48 + sx * 12 + 6], x2tt[:, c0:c0 + 512])
                                ev = inc(v.bn_stats(bnst[:, 48 + sx * 12 + 6:48 + sx * 12 + 12],
                                                    x2tt[:, c0 + 512:c0 + D]), "dve")
                                wge(v, ev)
                                ag2[sx] = inc(v.bn_aggr(mstat[:, 8 + 2 * sx:8 + 2 * sx + 2],
                                                        bnst[:, 48 + sx * 12:48 + sx * 12 + 12]), "dve")
                        sec("vector", v_st2)

                        def s_sq2(sc):
                            for sx in range(NSUB):
                                wge(sc, ag2[sx])
                                ag2[("q", sx)] = inc(sc.activation(out=rsq[:, 4 + sx:5 + sx],
                                                                   in_=mstat[:, 9 + 2 * sx:10 + 2 * sx],
                                                                   func=AF.Sqrt, bias=epsv[:]), "act")
                        sec("scalar", s_sq2)

                        def v_n2(v, tau=tau, ev_z2sp=ev_z2sp):
                            if tau > 0:
                                wge(v, ev_z2nb_war, ev_z2tm_war, ev_z2sp)
                            for sx in range(NSUB):
                                wge(v, ag2[("q", sx)])
                                ev = inc(v.reciprocal(rsi[:, 4 + sx:5 + sx], rsq[:, 4 + sx:5 + sx]), "dve")
                                wge(v, ev)
                                c0 = sx * D
                                ev = inc(v.tensor_scalar(out=z2tt[:, c0:c0 + D], in0=x2tt[:, c0:c0 + D],
                                                         scalar1=mstat[:, 8 + 2 * sx:9 + 2 * sx],
                                                         op0=ALU.subtract,
                                                         scalar2=rsi[:, 4 + sx:5 + sx], op1=ALU.mult), "dve")
                                wge(v, ev)
                                inc(v.tensor_copy(z2nb[:, c0:c0 + D], z2tt[:, c0:c0 + D]), "dve")
                        sec("vector", v_n2)
                        ev_z2done = now("dve")

                        def s_spz2(sy, r0=r0, ev_z2done=ev_z2done):
                            wge(sy, ev_z2done)
                            hold["z2sp"] = inc(sy.dma_start(
                                out=sp_z2[r0:r0 + NTOK, :].rearrange("(s p) j -> p s j", p=128),
                                in_=z2tt[:].rearrange("p (s j) -> p s j", j=D)), "qs", 16)
                        sec("sync", s_spz2)
                        ev_z2sp = hold["z2sp"]
                        ev_ld_free[par2] = ev_z2done

                        ev_z2nb = now("dve")

                        def s_spz2tm(sy, tau=tau, ev_z2nb=ev_z2nb):
                            wge(sy, ev_z2nb)
                            iv = sy.dma_start(
                                out=sp_z2tm[tau * NTOK:(tau + 1) * NTOK, :].rearrange("(s p) j -> p s j", p=128),
                                in_=z2nb[:].rearrange("p (s j) -> p s j", j=D))
                            hold_tm[0] = inc(iv, "qs", 16)
                        hold_tm = [None]
                        sec("sync", s_spz2tm)
                        ev_z2tm_war = hold_tm[0]

                        act_base = cnt["act"]
                        tr_ev = {}

                        def t_tr(t, tau=tau, ev_z2nb=ev_z2nb, act_base=act_base):
                            wge(t, ev_z2nb)
                            for ic in range(8):
                                bank = pT0 if ic % 2 == 0 else pT1
                                if ic >= 2:
                                    wge(t, ("act", act_base + ic - 1))
                                for s in range(NSUB):
                                    iv = t.transpose(bank[:, s * 128:(s + 1) * 128],
                                                     z2nb[:, s * D + ic * 128:s * D + ic * 128 + 128],
                                                     identb[:])
                                tr_ev[("t", ic)] = inc(iv, "pe")
                        sec("tensor", t_tr)
                        ev_z2nb_war = now("pe")

                        def s_tr(sc, tau=tau, ev_spfm_prev=ev_spfm_prev):
                            wge(sc, ev_spfm_prev)
                            for ic in range(8):
                                bank = pT0 if ic % 2 == 0 else pT1
                                wge(sc, tr_ev[("t", ic)])
                                iv = sc.copy(out=z2fm[:, ic * 512:(ic + 1) * 512], in_=bank[:])
                                inc(iv, "act")
                        sec("scalar", s_tr)
                        assert cnt["act"] == act_base + 8
                        ev_z2fm = now("act")

                        def s_spfm(sy, tau=tau, ev_z2fm=ev_z2fm):
                            wge(sy, ev_z2fm)
                            iv = sy.dma_start(
                                out=sp_z2fm.rearrange("(i p) t -> p i t", p=128)[:, :, tau * NTOK:(tau + 1) * NTOK],
                                in_=z2fm[:].rearrange("p (i t) -> p i t", t=512))
                            hold2[0] = inc(iv, "qs", 16)
                        hold2 = [None]
                        sec("sync", s_spfm)
                        ev_spfm_prev = hold2[0]

                        for s in range(NSUB):
                            st = tau * NSUB + s
                            rt = {}

                            def t_rt(t, s=s, st=st, ev_z2fm=ev_z2fm, rt_prev=dict(rt_prev)):
                                wge(t, ev_z2fm)
                                if st == 0:
                                    wge(t, prep_end["qp"])
                                if rt_prev:
                                    wge(t, rt_prev["evac"])
                                for ic in range(8):
                                    iv = t.matmul(pR[:], z2fm[:, ic * 512 + s * 128:ic * 512 + s * 128 + 128],
                                                  rw_sb[:, ic * NEXP:(ic + 1) * NEXP],
                                                  start=(ic == 0), stop=(ic == 7))
                                rt["mm"] = inc(iv, "pe")
                            sec("tensor", t_rt)

                            def v_rt1(v, st=st):
                                wge(v, rt["mm"])
                                ev = inc(v.tensor_tensor(out=logit[:], in0=pR[:], in1=rbbc[:], op=ALU.add), "dve")
                                rt["evac"] = ev
                                wge(v, ev)
                                ev = inc(v.tensor_reduce(out=sm1[:], in_=logit[:], axis=AX.X, op=ALU.max), "dve")
                                wge(v, ev)
                                inc(v.tensor_scalar(out=sm1[:], in0=sm1[:], scalar1=-1.0, scalar2=None,
                                                    op0=ALU.mult), "dve")
                            sec("vector", v_rt1)
                            ev_nm = now("dve")

                            def s_rt(sc, ev_nm=ev_nm):
                                wge(sc, ev_nm)
                                inc(sc.activation(out=expz[:], in_=logit[:], func=AF.Exp, bias=sm1[:]), "act")
                            sec("scalar", s_rt)
                            ev_ex = now("act")

                            def v_rt2(v, st=st, ev_ex=ev_ex):
                                wge(v, ev_ex)
                                ev = inc(v.tensor_reduce(out=sm2[:], in_=expz[:], axis=AX.X, op=ALU.add), "dve")
                                wge(v, ev)
                                ev = inc(v.reciprocal(sm2[:], sm2[:]), "dve")
                                wge(v, ev)
                                ev = inc(v.tensor_scalar(out=probs[:], in0=expz[:], scalar1=sm2[:],
                                                         scalar2=None, op0=ALU.mult), "dve")
                                wge(v, ev)
                                ev = inc(v.max(top8v[:], probs[:]), "dve")
                                wge(v, ev)
                                ev = inc(v.max_index(top8i[:], top8v[:], probs[:]), "dve")
                                wge(v, ev)
                                ev = inc(v.tensor_copy(idxf[:], top8i[:, 0:2]), "dve")
                                wge(v, ev)
                                inc(v.tensor_scalar(out=oh1s[:, st * NEXP:(st + 1) * NEXP], in0=iota16[:],
                                                    scalar1=idxf[:, 0:1], scalar2=None,
                                                    op0=ALU.is_equal), "dve")
                                ev = inc(v.tensor_scalar(out=oh2s[:, st * NEXP:(st + 1) * NEXP], in0=iota16[:],
                                                         scalar1=idxf[:, 1:2], scalar2=None,
                                                         op0=ALU.is_equal), "dve")
                                wge(v, ev)
                                inc(v.tensor_tensor(out=ohsumb[:, st * NEXP:(st + 1) * NEXP],
                                                    in0=oh1s[:, st * NEXP:(st + 1) * NEXP],
                                                    in1=oh2s[:, st * NEXP:(st + 1) * NEXP],
                                                    op=ALU.add), "dve")
                                ev = inc(v.tensor_tensor(out=sm1[:], in0=top8v[:, 0:1], in1=top8v[:, 1:2],
                                                         op=ALU.add), "dve")
                                wge(v, ev)
                                ev = inc(v.tensor_scalar(out=sm1[:], in0=sm1[:], scalar1=1e-9, scalar2=None,
                                                         op0=ALU.add), "dve")
                                wge(v, ev)
                                ev = inc(v.reciprocal(sm1[:], sm1[:]), "dve")
                                wge(v, ev)
                                inc(v.tensor_tensor(out=gstore[:, st * 2:st * 2 + 1], in0=top8v[:, 0:1],
                                                    in1=sm1[:], op=ALU.mult), "dve")
                                ev = inc(v.tensor_tensor(out=gstore[:, st * 2 + 1:st * 2 + 2],
                                                         in0=top8v[:, 1:2], in1=sm1[:], op=ALU.mult), "dve")
                                rt["oh"] = ev
                            sec("vector", v_rt2)

                            def t_cnt(t, st=st):
                                wge(t, rt["oh"])
                                iv = t.matmul(pCNT[:], onesbc[:], ohsumb[:, st * NEXP:(st + 1) * NEXP],
                                              start=(st == 0), stop=(st == 31))
                                inc(iv, "pe")
                            sec("tensor", t_cnt)
                            rt_prev = rt

                    ev_cnt_done = now("pe")

                    def v_cnt(v):
                        wge(v, ev_cnt_done)
                        inc(v.tensor_copy(cntrow[:], pCNT[:]), "dve")
                    sec("vector", v_cnt)
                    ev_cntrow = now("dve")

                    def g_cc(g):
                        wge(g, ev_cntrow)
                        inc(g.dma_start(out=cc_in[:, :], in_=cntrow[:]), "qp", 16)
                        wge(g, now("qp"))
                        iv = g.collective_compute(
                            "AllGather", ALU.bypass,
                            replica_groups=[[0, 1, 2, 3, 4, 5, 6, 7]],
                            ins=[cc_in[:, :].opt()],
                            outs=[cc_out[:, :].opt()])
                        inc(iv, "cc")
                        wge(g, now("cc"))
                        inc(g.dma_start(out=cnts_all[:], in_=cc_out[:, :]), "qp", 16)
                    sec("gpsimd", g_cc)
                    ev_cnts = now("qp")

                    def t_off(t):
                        wge(t, ev_cnts, prep_end["qs"])
                        inc(t.matmul(pR[0:1, :], rankpre[:], cnts_all[:], start=True, stop=True), "pe")
                    sec("tensor", t_off)
                    ev_offmm = now("pe")

                    def v_off(v):
                        wge(v, ev_offmm)
                        ev = inc(v.tensor_copy(offs[:], pR[0:1, :]), "dve")
                        wge(v, ev)
                        inc(v.tensor_copy(base[:], offs[:]), "dve")
                    sec("vector", v_off)
                    ev_base0 = now("dve")

                    if "tap_small" in taps:
                        def s_tap(sy):
                            wge(sy, ev_base0)
                            inc(sy.dma_start(out=tap_small[0:1, 0:NEXP], in_=cntrow[:]), "qs", 16)
                            inc(sy.dma_start(out=tap_small[1:2, 0:NEXP], in_=offs[:]), "qs", 16)
                            inc(sy.dma_start(out=tap_small[2:3, 0:NEXP],
                                             in_=cnts_all[0:1, :]), "qs", 16)
                        sec("sync", s_tap)


                    p2b_end = {k: now(k) for k in sem}

            if phases >= 3:
              phase_barrier()
              with contextlib.ExitStack() as p3:
                a = lambda name, shape, dt: p3.enter_context(nc.sbuf_tensor(_un(name), shape, dt))
                wg3 = a("wg3", [128, 8 * FF], BF16)
                wu3 = a("wu3", [128, 8 * FF], BF16)
                wd3 = a("wd3", [128, 11 * D], BF16)
                zgt = a("zgt", [128, NSUB * D], BF16)
                zgf = a("zgf", [128, NSUB * D], BF16)
                hid3 = a("hid3", [128, 11 * 512], BF16)
                silu_s = a("silu_s", [128, 1024], BF16)
                orow = a("orow", [128, NSUB * D], BF16)
                tabf = a("tabf", [128, 80], F32)
                tabu = a("tabu", [128, 80], U32)
                ztile = a("ztile", [128, 160], F32)
                gth = [a("gth", [128, 2 * D], BF16) for _ in range(2)]
                accL = a("accL", [128, D], BF16)
                z2l = a("z2l", [128, D], F32)
                x2l = a("x2l", [128, D], F32)
                offn = a("offn", [128, D], F32)
                outs4 = a("outs4", [128, D], F32)
                with contextlib.ExitStack() as pst:
                    ps = lambda name, shape, dt=F32: pst.enter_context(nc.psum_tensor(_un(name), shape, dt))
                    pT0 = ps("pT0", [128, 512], BF16)
                    pT1 = ps("pT1", [128, 512], BF16)
                    pU = [ps(f"pU{i}", [128, 512]) for i in range(4)]
                    pD0 = ps("pD0", [128, 512])
                    pD1 = ps("pD1", [128, 512])

                    p3_start = {k: now(k) for k in sem}

                    def v_zt(v):
                        iv = v.memset(ztile[:], 0.0)
                        inc(iv, "dve")
                        iv = v.memset(gth[0][:], 0.0)
                        inc(iv, "dve")
                        iv = v.memset(gth[1][:], 0.0)
                        inc(iv, "dve")
                    sec("vector", v_zt)
                    ev_zt = now("dve")

                    bndregs = {}

                    def g_kick(g):
                        bndregs["tc"] = g.alloc_register("bnd_tc")
                        bndregs["z2"] = g.alloc_register("bnd_z2")
                        bndregs["al"] = g.alloc_register("bnd_al")
                        g.reg_mov(bndregs["tc"], 16 * 128 * 40 - 1)
                        g.reg_mov(bndregs["z2"], 8 * S - 1)
                        g.reg_mov(bndregs["al"], 16 * CAP + 128 - 1)
                        inc(g.dma_start(out=wg3[:].rearrange("p (i j) -> p i j", j=FF),
                                        in_=swg_h.rearrange("(i p) j -> p i j", p=128)), "qw", 16)
                        inc(g.dma_start(out=wu3[:].rearrange("p (i j) -> p i j", j=FF),
                                        in_=swu_h.rearrange("(i p) j -> p i j", p=128)), "qw", 16)
                        inc(g.dma_start(out=wd3[:].rearrange("p (i j) -> p i j", j=D),
                                        in_=swd_h.rearrange("(i p) j -> p i j", p=128)), "qw", 16)
                        wge(g, ev_zt)
                        for i in range(4):
                            inc(g.dma_start(
                                out=t_c[i * 20480:(i + 1) * 20480, :].rearrange("(p a) k -> p a k", p=128),
                                in_=ztile[:, 0:160].rearrange("p (a k) -> p a k", k=1)), "qp", 16)
                        inc(g.dma_start(out=all2[16 * CAP:16 * CAP + 128, 0:512], in_=zerosb[:]), "qp", 16)
                        inc(g.dma_start(out=all2[16 * CAP:16 * CAP + 128, 512:1024], in_=zerosb[:]), "qp", 16)
                        iv = g.collective_compute(
                            "AllGather", ALU.bypass,
                            replica_groups=[[0, 1, 2, 3, 4, 5, 6, 7]],
                            ins=[sp_z2tm[:, :].opt()], outs=[z2_all[:, :].opt()])
                        inc(iv, "cc")
                    sec("gpsimd", g_kick)
                    ev_tabf = None
                    ev_cc_z2 = ("cc", cnt["cc"] - 1)
                    ev_cc_rs = ("cc", cnt["cc"])

                    if True:
                        pR = pD0
                        pBCB = pD1
                        pCNT = pU[0]
                        ev_gate = None
                        for st in range(32):
                            g_ev = {}

                            def t_g(t, st=st, ev_gate=ev_gate):
                                wge(t, ev_gate)
                                iv = t.matmul(pR[:, 0:NEXP], trius[:], ohsumb[:, st * NEXP:(st + 1) * NEXP],
                                              start=True, stop=True)
                                g_ev["cum"] = inc(iv, "pe")
                                iv = t.matmul(pBCB[:, 0:NEXP], onesf[:], base[:], start=True, stop=True)
                                g_ev["bc"] = inc(iv, "pe")
                                iv = t.matmul(pCNT[0:1, 0:NEXP], onesbc[:], ohsumb[:, st * NEXP:(st + 1) * NEXP],
                                              start=True, stop=True)
                                g_ev["cnt"] = inc(iv, "pe")
                            sec("tensor", t_g)

                            def v_g(v, st=st):
                                wge(v, g_ev["cum"], g_ev["bc"])
                                inc(v.scalar_tensor_tensor(out=scr16[:, 0:NEXP], in0=pR[:, 0:NEXP], scalar=1.0,
                                                           in1=oh1s[:, st * NEXP:(st + 1) * NEXP],
                                                           op0=ALU.mult, op1=ALU.mult,
                                                           accum_out=pos1[:]), "dve")
                                inc(v.scalar_tensor_tensor(out=scr16[:, NEXP:2 * NEXP], in0=pR[:, 0:NEXP], scalar=1.0,
                                                           in1=oh2s[:, st * NEXP:(st + 1) * NEXP],
                                                           op0=ALU.mult, op1=ALU.mult,
                                                           accum_out=pos2[:]), "dve")
                                inc(v.scalar_tensor_tensor(out=scr16[:, 2 * NEXP:3 * NEXP], in0=pBCB[:, 0:NEXP], scalar=1.0,
                                                           in1=oh1s[:, st * NEXP:(st + 1) * NEXP],
                                                           op0=ALU.mult, op1=ALU.mult,
                                                           accum_out=bv1[:]), "dve")
                                ev = inc(v.scalar_tensor_tensor(out=scr16[:, 3 * NEXP:4 * NEXP], in0=pBCB[:, 0:NEXP], scalar=1.0,
                                                                in1=oh2s[:, st * NEXP:(st + 1) * NEXP],
                                                                op0=ALU.mult, op1=ALU.mult,
                                                                accum_out=bv2[:]), "dve")
                                wge(v, ev)
                                inc(v.tensor_tensor(out=pos1[:], in0=pos1[:], in1=bv1[:], op=ALU.add), "dve")
                                ev = inc(v.tensor_tensor(out=pos2[:], in0=pos2[:], in1=bv2[:], op=ALU.add), "dve")
                                wge(v, ev)
                                inc(v.tensor_scalar(out=w1c[:], in0=pos1[:], scalar1=float(CAP - 1),
                                                    scalar2=None, op0=ALU.is_le), "dve")
                                inc(v.tensor_scalar(out=w2c[:], in0=pos2[:], scalar1=float(CAP - 1),
                                                    scalar2=None, op0=ALU.is_le), "dve")
                                inc(v.tensor_scalar(out=bv1[:], in0=pos1[:], scalar1=float(CAP - 1),
                                                    scalar2=None, op0=ALU.min), "dve")
                                ev = inc(v.tensor_scalar(out=bv2[:], in0=pos2[:], scalar1=float(CAP - 1),
                                                         scalar2=None, op0=ALU.min), "dve")
                                wge(v, ev)
                                inc(v.tensor_tensor(out=gwst[:, st * 2:st * 2 + 1], in0=w1c[:],
                                                    in1=gstore[:, st * 2:st * 2 + 1], op=ALU.mult), "dve")
                                inc(v.tensor_tensor(out=gwst[:, st * 2 + 1:st * 2 + 2], in0=w2c[:],
                                                    in1=gstore[:, st * 2 + 1:st * 2 + 2], op=ALU.mult), "dve")
                                inc(v.scalar_tensor_tensor(out=scr16[:, 4 * NEXP:5 * NEXP], in0=halfbc[:], scalar=1.0,
                                                           in1=oh1s[:, st * NEXP:(st + 1) * NEXP],
                                                           op0=ALU.mult, op1=ALU.mult, accum_out=aux[:, 0:1]), "dve")
                                inc(v.scalar_tensor_tensor(out=scr16[:, 5 * NEXP:6 * NEXP], in0=halfbc[:], scalar=1.0,
                                                           in1=oh2s[:, st * NEXP:(st + 1) * NEXP],
                                                           op0=ALU.mult, op1=ALU.mult, accum_out=aux[:, 1:2]), "dve")
                                inc(v.scalar_tensor_tensor(out=scr16[:, 6 * NEXP:7 * NEXP], in0=parbc[:], scalar=1.0,
                                                           in1=oh1s[:, st * NEXP:(st + 1) * NEXP],
                                                           op0=ALU.mult, op1=ALU.mult, accum_out=aux[:, 2:3]), "dve")
                                ev = inc(v.scalar_tensor_tensor(out=scr16[:, 7 * NEXP:8 * NEXP], in0=parbc[:], scalar=1.0,
                                                                in1=oh2s[:, st * NEXP:(st + 1) * NEXP],
                                                                op0=ALU.mult, op1=ALU.mult, accum_out=aux[:, 3:4]), "dve")
                                inc(v.scalar_tensor_tensor(out=scr16[:, 8 * NEXP:9 * NEXP], in0=iota16[:], scalar=1.0,
                                                           in1=oh1s[:, st * NEXP:(st + 1) * NEXP],
                                                           op0=ALU.mult, op1=ALU.mult, accum_out=aux[:, 8:9]), "dve")
                                ev = inc(v.scalar_tensor_tensor(out=scr16[:, 9 * NEXP:10 * NEXP], in0=iota16[:], scalar=1.0,
                                                                in1=oh2s[:, st * NEXP:(st + 1) * NEXP],
                                                                op0=ALU.mult, op1=ALU.mult, accum_out=aux[:, 9:10]), "dve")
                                wge(v, ev)
                                for k in range(2):
                                    posk = pos1 if k == 0 else pos2
                                    wnk = w1c if k == 0 else w2c
                                    pck = bv1 if k == 0 else bv2
                                    hk = aux[:, k:k + 1]
                                    pk = aux[:, 2 + k:3 + k]
                                    idk = aux[:, 8 + k:9 + k]
                                    ev = inc(v.tensor_copy(auxu[:, 0:1], posk[:]), "dve")
                                    wge(v, ev)
                                    ev = inc(v.tensor_scalar(out=auxu[:, 1:2], in0=auxu[:, 0:1],
                                                             scalar1=127, scalar2=None,
                                                             op0=ALU.bitwise_and), "dve")
                                    wge(v, ev)
                                    ev = inc(v.tensor_copy(aux[:, 4:5], auxu[:, 1:2]), "dve")
                                    wge(v, ev)
                                    ev = inc(v.scalar_tensor_tensor(out=aux[:, 5:6], in0=aux[:, 4:5],
                                                                    scalar=-1.0, in1=posk[:],
                                                                    op0=ALU.mult, op1=ALU.add), "dve")
                                    wge(v, ev)
                                    ev = inc(v.tensor_scalar(out=aux[:, 5:6], in0=aux[:, 5:6],
                                                             scalar1=0.0078125, scalar2=None,
                                                             op0=ALU.mult), "dve")
                                    wge(v, ev)
                                    ev = inc(v.tensor_scalar(out=aux[:, 6:7], in0=idk, scalar1=5120.0,
                                                             scalar2=None, op0=ALU.mult), "dve")
                                    wge(v, ev)
                                    ev = inc(v.scalar_tensor_tensor(out=aux[:, 6:7], in0=aux[:, 4:5],
                                                                    scalar=40.0, in1=aux[:, 6:7],
                                                                    op0=ALU.mult, op1=ALU.add), "dve")
                                    wge(v, ev)
                                    ev = inc(v.tensor_tensor(out=aux[:, 6:7], in0=aux[:, 6:7],
                                                             in1=aux[:, 5:6], op=ALU.add), "dve")
                                    wge(v, ev)
                                    ev = inc(v.tensor_scalar(out=aux[:, 6:7], in0=aux[:, 6:7],
                                                             scalar1=wnk[:], scalar2=None, op0=ALU.mult), "dve")
                                    ev2 = inc(v.tensor_scalar(out=aux[:, 7:8], in0=wnk[:], scalar1=-1.0,
                                                              scalar2=1.0, op0=ALU.mult, op1=ALU.add), "dve")
                                    wge(v, ev, ev2)
                                    ev = inc(v.scalar_tensor_tensor(out=aux[:, 6:7], in0=oobcol[:],
                                                                    scalar=aux[:, 7:8], in1=aux[:, 6:7],
                                                                    op0=ALU.mult, op1=ALU.add), "dve")
                                    wge(v, ev)
                                    inc(v.tensor_copy(sidxu[:, st * 2 + k:st * 2 + k + 1], aux[:, 6:7]), "dve")
                                    ev = inc(v.scalar_tensor_tensor(out=aux[:, 4:5], in0=hk,
                                                                    scalar=5120.0, in1=pck[:],
                                                                    op0=ALU.mult, op1=ALU.add), "dve")
                                    wge(v, ev)
                                    ev = inc(v.scalar_tensor_tensor(out=aux[:, 4:5], in0=pk,
                                                                    scalar=float(8 * CAP), in1=aux[:, 4:5],
                                                                    op0=ALU.mult, op1=ALU.add), "dve")
                                    wge(v, ev)
                                    ev = inc(v.tensor_scalar(out=aux[:, 4:5], in0=aux[:, 4:5],
                                                             scalar1=wnk[:], scalar2=None,
                                                             op0=ALU.mult), "dve")
                                    ev2 = inc(v.tensor_scalar(out=aux[:, 5:6], in0=zrowcol[:],
                                                              scalar1=aux[:, 7:8], scalar2=None,
                                                              op0=ALU.mult), "dve")
                                    wge(v, ev, ev2)
                                    ev = inc(v.tensor_tensor(out=aux[:, 4:5], in0=aux[:, 4:5],
                                                             in1=aux[:, 5:6], op=ALU.add), "dve")
                                    wge(v, ev)
                                    ev = inc(v.tensor_copy(cidxu[:, st * 2 + k:st * 2 + k + 1],
                                                           aux[:, 4:5]), "dve")
                                    wge(v, ev)
                                ev = inc(v.tensor_scalar(out=aux[:, 4:5], in0=iotacol[:], scalar1=1.0,
                                                         scalar2=float(st * 128), op0=ALU.mult,
                                                         op1=ALU.add), "dve")
                                wge(v, ev)
                                ev = inc(v.tensor_tensor(out=aux[:, 4:5], in0=aux[:, 4:5], in1=rankvec[:],
                                                         op=ALU.add), "dve")
                                wge(v, ev)
                                inc(v.tensor_copy(tokidu[:, st:st + 1], aux[:, 4:5]), "dve")
                                wge(v, g_ev["cnt"])
                                inc(v.tensor_tensor(out=base[:], in0=base[:], in1=pCNT[0:1, 0:NEXP], op=ALU.add), "dve")
                            sec("vector", v_g)
                            ev_gate = now("dve")
                        ev_275_tail = now("dve")

                    def g_kick2(g):
                        wge(g, ev_275_tail)
                        wge(g, now("qp"))
                        for st in range(32):
                            for k in range(2):
                                if (st * 2 + k) % 4 == 0:
                                    wge(g, now("qp"))
                                iv = g.indirect_dma_start(
                                    out=t_c,
                                    out_offset=bass.IndirectOffsetOnAxis(
                                        ap=sidxu[:, st * 2 + k:st * 2 + k + 1], axis=0),
                                    in_=tokidu[:, st:st + 1], in_offset=None,
                                    bounds_check=bndregs["tc"], oob_is_err=False)
                                inc(iv, "qp", 16)
                        wge(g, now("qp"))
                        iv = g.collective_compute(
                            "ReduceScatter", ALU.add,
                            replica_groups=[[0, 1, 2, 3, 4, 5, 6, 7]],
                            ins=[t_c[:, :].opt()], outs=[t_own[:, :].opt()])
                        inc(iv, "cc")
                        wge(g, now("cc"))
                        inc(g.dma_start(out=tabf[:].rearrange("p (e j) -> p e j", j=40),
                                        in_=t_own.rearrange("(e p j) k -> p e (j k)", p=128, j=40)),
                            "qp", 16)
                    sec("gpsimd", g_kick2)
                    ev_tabf = now("qp")

                    def v_tab(v):
                        wge(v, ev_tabf)
                        inc(v.tensor_copy(tabu[:], tabf[:]), "dve")
                    sec("vector", v_tab)
                    ev_tabu = now("dve")

                    prev_up_tail = [None, ev_275_tail]
                    prev_dn_tail = [None, ev_275_tail]
                    prev_tr_tail = [None]
                    prev_store = [None]

                    def emit_mlp_tile(src_fm, ev_srcs, out_rows, out_q):
                        act_base = cnt["act"]
                        dve_base = cnt["dve"]
                        up_ev = {}

                        def t_up(t, up_tail=tuple(prev_up_tail), ev_srcs=tuple(ev_srcs),
                                 act_base=act_base, dve_base=dve_base):
                            wge(t, *ev_srcs)
                            wge(t, up_tail[0], up_tail[1])
                            for oc in range(11):
                                b0 = pU[(oc % 2) * 2]
                                b1 = pU[(oc % 2) * 2 + 1]
                                if oc >= 2:
                                    wge(t, ("act", act_base + oc - 1))
                                    wge(t, ("dve", dve_base + oc - 1))
                                for ic in range(8):
                                    iv = t.matmul(b0[:], wg3[:, ic * FF + oc * 128:ic * FF + oc * 128 + 128],
                                                  src_fm[:, ic * 512:(ic + 1) * 512],
                                                  start=(ic == 0), stop=(ic == 7))
                                up_ev[("g", oc)] = inc(iv, "pe")
                                for ic in range(8):
                                    iv = t.matmul(b1[:], wu3[:, ic * FF + oc * 128:ic * FF + oc * 128 + 128],
                                                  src_fm[:, ic * 512:(ic + 1) * 512],
                                                  start=(ic == 0), stop=(ic == 7))
                                up_ev[("u", oc)] = inc(iv, "pe")
                        sec("tensor", t_up)

                        def s_up(sc, dve_base=dve_base):
                            for oc in range(11):
                                b0 = pU[(oc % 2) * 2]
                                wge(sc, up_ev[("g", oc)])
                                if oc >= 2:
                                    wge(sc, ("dve", dve_base + oc - 1))
                                iv = sc.activation(out=silu_s[:, (oc % 2) * 512:(oc % 2) * 512 + 512],
                                                   in_=b0[:], func=AF.Silu)
                                up_ev[("s", oc)] = inc(iv, "act")
                        sec("scalar", s_up)
                        assert cnt["act"] == act_base + 11

                        def v_up(v):
                            for oc in range(11):
                                b1 = pU[(oc % 2) * 2 + 1]
                                wge(v, up_ev[("s", oc)], up_ev[("u", oc)])
                                iv = v.tensor_tensor(out=hid3[:, oc * 512:(oc + 1) * 512],
                                                     in0=silu_s[:, (oc % 2) * 512:(oc % 2) * 512 + 512],
                                                     in1=b1[:], op=ALU.mult)
                                inc(iv, "dve")
                        sec("vector", v_up)
                        assert cnt["dve"] == dve_base + 11
                        prev_up_tail[0] = now("act")
                        prev_up_tail[1] = now("dve")
                        ev_hid3 = now("dve")

                        dve_base = cnt["dve"]
                        act_base = cnt["act"]
                        dn_ev = {}

                        def t_dn(t, ev_hid3=ev_hid3, dve_base=dve_base, act_base=act_base,
                                 dn_tail=tuple(prev_dn_tail), pstore=prev_store[0]):
                            wge(t, ev_hid3)
                            wge(t, dn_tail[0], dn_tail[1], pstore)
                            pb = 0
                            for sx in range(NSUB):
                                for half in range(2):
                                    bank = pD0 if pb % 2 == 0 else pD1
                                    if pb >= 2:
                                        if pb % 2 == 0:
                                            wge(t, ("dve", dve_base + pb // 2))
                                        else:
                                            wge(t, ("act", act_base + (pb - 1) // 2))
                                    for oc in range(11):
                                        iv = t.matmul(bank[:],
                                                      hid3[:, oc * 512 + sx * 128:oc * 512 + sx * 128 + 128],
                                                      wd3[:, oc * D + half * 512:oc * D + half * 512 + 512],
                                                      start=(oc == 0), stop=(oc == 10))
                                    dn_ev[("m", pb)] = inc(iv, "pe")
                                    pb += 1
                        sec("tensor", t_dn)

                        def v_dn(v, dve_base=dve_base):
                            for sx in range(NSUB):
                                pb = sx * 2
                                wge(v, dn_ev[("m", pb)])
                                iv = v.tensor_copy(orow[:, sx * D:sx * D + 512], pD0[:])
                                inc(iv, "dve")
                        sec("vector", v_dn)

                        def s_dn(sc, act_base=act_base):
                            for sx in range(NSUB):
                                pb = sx * 2 + 1
                                wge(sc, dn_ev[("m", pb)])
                                iv = sc.copy(out=orow[:, sx * D + 512:(sx + 1) * D], in_=pD1[:])
                                inc(iv, "act")
                        sec("scalar", s_dn)
                        prev_dn_tail[0] = now("act")
                        prev_dn_tail[1] = now("dve")
                        ev_or = (now("dve"), now("act"))

                        hold = [None]

                        def s_st(sy, ev_or=ev_or):
                            wge(sy, ev_or[0], ev_or[1])
                            hold[0] = inc(sy.dma_start(
                                out=out_rows.rearrange("(s p) j -> p s j", p=128),
                                in_=orow[:].rearrange("p (s j) -> p s j", j=D)), out_q, 16)
                        sec("sync", s_st)
                        prev_store[0] = hold[0]
                        return hold[0]

                    ev_w_sh = now("qw")
                    ev_zg_free = None
                    for tau in range(NT):
                        holdz = [None]

                        def s_z(sy, tau=tau, ev_zg_free=ev_zg_free):
                            wge(sy, ev_zg_free)
                            iv = sy.dma_start(
                                out=zgf[:].rearrange("p (i t) -> p i t", t=512),
                                in_=sp_z2fm.rearrange("(i p) t -> p i t", p=128)[:, :, tau * NTOK:(tau + 1) * NTOK])
                            holdz[0] = inc(iv, "qs", 16)
                        sec("sync", s_z)
                        emit_mlp_tile(zgf, (holdz[0], ev_w_sh if tau == 0 else None),
                                      sp_acc[tau * NTOK:(tau + 1) * NTOK, :], "qs")
                        ev_zg_free = now("pe")

                    def g_shecho(g):
                        wge(g, now("qs"))
                        inc(g.engine_nop(), "pool")
                    sec("gpsimd", g_shecho)
                    ev_shared_done = now("pool")

                    ag_done = {}
                    for eps in range(2):
                        wsrcs = (wgo_h[eps], wuo_h[eps], wdo_h[eps])
                        ev_wread = now("pe")

                        def g_we(g, eps=eps, wsrcs=wsrcs, ev_wread=ev_wread):
                            wge(g, ev_wread)
                            inc(g.dma_start(out=wg3[:].rearrange("p (i j) -> p i j", j=FF),
                                            in_=wsrcs[0].rearrange("(i p) j -> p i j", p=128)), "qw", 16)
                            inc(g.dma_start(out=wu3[:].rearrange("p (i j) -> p i j", j=FF),
                                            in_=wsrcs[1].rearrange("(i p) j -> p i j", p=128)), "qw", 16)
                            inc(g.dma_start(out=wd3[:].rearrange("p (i j) -> p i j", j=D),
                                            in_=wsrcs[2].rearrange("(i p) j -> p i j", p=128)), "qw", 16)
                        sec("gpsimd", g_we)
                        ev_we = now("qw")

                        outX = outA if eps == 0 else outB
                        allX = all2[0:8 * CAP, :] if eps == 0 else all2[8 * CAP:16 * CAP, :]
                        last_store = None
                        for jt in range(10):
                            holdg = [None]

                            def g_rows(g, eps=eps, jt=jt):
                                wge(g, ev_tabu, ev_cc_z2)
                                wge(g, prev_tr_tail[0])
                                wge(g, now("qp"))
                                for i in range(4):
                                    col = eps * 40 + jt * 4 + i
                                    iv = g.indirect_dma_start(
                                        out=zgt[:, i * D:(i + 1) * D],
                                        out_offset=None,
                                        in_=z2_all,
                                        in_offset=bass.IndirectOffsetOnAxis(
                                            ap=tabu[:, col:col + 1], axis=0),
                                        bounds_check=bndregs["z2"], oob_is_err=False)
                                    inc(iv, "qp", 16)
                            sec("gpsimd", g_rows)
                            ev_rows = now("qp")

                            act_base = cnt["act"]
                            tr_ev = {}

                            def t_tr(t, ev_rows=ev_rows, act_base=act_base, eps=eps, jt=jt):
                                wge(t, ev_rows)
                                if jt == 0:
                                    wge(t, ev_cc_z2)
                                for ic in range(8):
                                    bank = pT0 if ic % 2 == 0 else pT1
                                    if ic >= 2:
                                        wge(t, ("act", act_base + ic - 1))
                                    for sx in range(NSUB):
                                        iv = t.transpose(bank[:, sx * 128:(sx + 1) * 128],
                                                         zgt[:, sx * D + ic * 128:sx * D + ic * 128 + 128],
                                                         identb[:])
                                    tr_ev[("t", ic)] = inc(iv, "pe")
                            sec("tensor", t_tr)
                            prev_tr_tail[0] = now("pe")

                            def s_tr(sc):
                                for ic in range(8):
                                    bank = pT0 if ic % 2 == 0 else pT1
                                    wge(sc, tr_ev[("t", ic)])
                                    iv = sc.copy(out=zgf[:, ic * 512:(ic + 1) * 512], in_=bank[:])
                                    inc(iv, "act")
                            sec("scalar", s_tr)
                            assert cnt["act"] == act_base + 8
                            ev_zgf = now("act")
                            st_ev = emit_mlp_tile(zgf, (ev_zgf, ev_we if jt == 0 else None),
                                                  outX[jt * 512:(jt + 1) * 512, :], "qs")
                            last_store = st_ev

                        def g_ag(g, eps=eps, allX=allX, outX=outX):
                            wge(g, now("qs"))
                            iv = g.collective_compute(
                                "AllGather", ALU.bypass,
                                replica_groups=[[0, 1, 2, 3, 4, 5, 6, 7]],
                                ins=[outX[:, :].opt()], outs=[allX.opt()])
                            inc(iv, "cc")
                        sec("gpsimd", g_ag)
                        ag_done[eps] = now("cc")

                    p3_end = {k: now(k) for k in sem}

                    if phases >= 4:
                        ev_buf_free = None
                        ev_y_prev = None
                        ev_gth_free = [None, None]
                        for g_i in range(32):
                            gbuf = gth[g_i % 2]
                            r0 = g_i * 128
                            f_ev = {}

                            def s_l4(sy, g_i=g_i, r0=r0, ev_buf_free=ev_buf_free):
                                wge(sy, ev_shared_done)
                                wge(sy, ev_buf_free)
                                inc(sy.dma_start(out=z2l[:], in_=sp_z2[r0:r0 + 128, :]), "qa", 16)
                                inc(sy.dma_start(out=x2l[:], in_=sp_x2[r0:r0 + 128, :]), "qa", 16)
                                f_ev["ld"] = inc(sy.dma_start(out=accL[:], in_=sp_acc[r0:r0 + 128, :]), "qa", 16)
                            sec("scalar", s_l4)

                            def g_c4(g, g_i=g_i, gbuf=gbuf, ev_gth_free=ev_gth_free[g_i % 2]):
                                wge(g, ag_done[0], ag_done[1])
                                wge(g, ev_gth_free)
                                wge(g, now("qp"))
                                for k in range(2):
                                    iv = g.indirect_dma_start(
                                        out=gbuf[:, k * D:(k + 1) * D],
                                        out_offset=None,
                                        in_=all2,
                                        in_offset=bass.IndirectOffsetOnAxis(
                                            ap=cidxu[:, g_i * 2 + k:g_i * 2 + k + 1],
                                            axis=0),
                                        bounds_check=bndregs["al"], oob_is_err=False)
                                    inc(iv, "qp", 16)
                                f_ev["gth"] = now("qp")
                            sec("gpsimd", g_c4)

                            def v_c4(v, g_i=g_i, gbuf=gbuf):
                                wge(v, f_ev["ld"], f_ev["gth"])
                                ev = inc(v.tensor_scalar(out=offn[:], in0=gbuf[:, 0:D],
                                                         scalar1=gwst[:, g_i * 2:g_i * 2 + 1],
                                                         scalar2=None, op0=ALU.mult), "dve")
                                wge(v, ev)
                                ev = inc(v.scalar_tensor_tensor(out=offn[:], in0=gbuf[:, D:2 * D],
                                                                scalar=gwst[:, g_i * 2 + 1:g_i * 2 + 2],
                                                                in1=offn[:], op0=ALU.mult, op1=ALU.add), "dve")
                                wge(v, ev)
                                ev = inc(v.tensor_tensor(out=offn[:], in0=offn[:], in1=z2l[:], op=ALU.add), "dve")
                                wge(v, ev)
                                ev = inc(v.tensor_tensor(out=offn[:], in0=offn[:], in1=accL[:], op=ALU.add), "dve")
                                f_ev["gthdone"] = ev
                                wge(v, ev)
                                v.bn_stats(bnst[:, 0:6], offn[:, 0:512])
                                ev = inc(v.bn_stats(bnst[:, 6:12], offn[:, 512:D]), "dve")
                                wge(v, ev)
                                f_ev["agg"] = inc(v.bn_aggr(mstat[:, 0:2], bnst[:, 0:12]), "dve")
                            sec("vector", v_c4)
                            ev_gth_free[g_i % 2] = f_ev["gthdone"]

                            def s_c4(sc, g_i=g_i):
                                wge(sc, f_ev["agg"])
                                iv = sc.activation(out=rsq[:, 0:1], in_=mstat[:, 1:2], func=AF.Sqrt,
                                                   bias=epsv[:])
                                f_ev["sq"] = inc(iv, "act")
                            sec("scalar", s_c4)

                            def v_f4(v, g_i=g_i, ev_y_prev=ev_y_prev):
                                wge(v, f_ev["sq"], ev_y_prev)
                                ev = inc(v.reciprocal(rsi[:, 0:1], rsq[:, 0:1]), "dve")
                                wge(v, ev)
                                ev = inc(v.tensor_scalar(out=outs4[:], in0=offn[:], scalar1=mstat[:, 0:1],
                                                         op0=ALU.subtract, scalar2=rsi[:, 0:1],
                                                         op1=ALU.mult), "dve")
                                wge(v, ev)
                                f_ev["out"] = inc(v.tensor_tensor(out=outs4[:], in0=outs4[:], in1=x2l[:],
                                                                  op=ALU.add), "dve")
                            sec("vector", v_f4)
                            ev_buf_free = f_ev["out"]

                            hold4 = [None]

                            def s_o4(sy, g_i=g_i, r0=r0):
                                wge(sy, f_ev["out"])
                                hold4[0] = inc(sy.dma_start(out=y_h[r0:r0 + 128, :], in_=outs4[:]), "qa", 16)
                            sec("scalar", s_o4)
                            ev_y_prev = hold4[0]

            def s_fin(sy):
                sy.wait_ge(sem["qs"], cnt["qs"])
                sy.wait_ge(sem["qp"], cnt["qp"])
                sy.wait_ge(sem["qa"], cnt["qa"])
                sy.wait_ge(sem["qw"], cnt["qw"])
            sec("sync", s_fin)

    return nc


def kernel(**inputs):
    x = np.ascontiguousarray(inputs["L0_attn_x"])
    taps = tuple(DEBUG_TAPS)
    nc = build_program(taps, BUILD_PHASES)
    bf = ml_dtypes.bfloat16
    shared = {
        "wq": inputs["L0_attn_wq"], "wk": inputs["L0_attn_wk"],
        "wv": inputs["L0_attn_wv"], "wo": inputs["L0_attn_wo"],
        "ln1g": inputs["L0_attn_ln1_g"], "ln1b": inputs["L0_attn_ln1_b"],
        "ln2g": inputs["L0_attn_ln2_g"], "ln2b": inputs["L0_attn_ln2_b"],
        "cw1": inputs["L0_attn_cms_w1"], "cb1": inputs["L0_attn_cms_b1"],
        "cw2": inputs["L0_attn_cms_w2"], "cb2": inputs["L0_attn_cms_b2"],
        "rw": inputs["L0_ffn_router_w"], "rb": inputs["L0_ffn_router_b"],
        "wg": inputs["L0_ffn_w_gate"], "wu": inputs["L0_ffn_w_up"],
        "wd": inputs["L0_ffn_w_down"],
        "swg": inputs["L0_ffn_sw_gate"], "swu": inputs["L0_ffn_sw_up"],
        "swd": inputs["L0_ffn_sw_down"],
        "identb_h": np.eye(128, dtype=bf),
        "identf_h": np.eye(128, dtype=np.float32),
        "maskt_h": np.triu(np.ones((128, 128), dtype=np.float32)),
        "trius_h": np.triu(np.ones((128, 128)), 1).astype(bf),
    }
    halfpar = np.concatenate([np.arange(16) // 2, np.arange(16) % 2]).astype(np.float32).reshape(1, 32)
    wg_full = inputs["L0_ffn_w_gate"]
    wu_full = inputs["L0_ffn_w_up"]
    wd_full = inputs["L0_ffn_w_down"]
    in_maps = []
    for c in range(8):
        m = dict(shared)
        m["x"] = np.ascontiguousarray(x[c])
        m["rankpre_h"] = (np.arange(8) < c).astype(np.float32).reshape(8, 1)
        m["rankvec_h"] = np.full((128, 1), c * 4096, np.float32)
        m["halfpar"] = halfpar
        m["wgo"] = np.ascontiguousarray(wg_full[2 * c:2 * c + 2])
        m["wuo"] = np.ascontiguousarray(wu_full[2 * c:2 * c + 2])
        m["wdo"] = np.ascontiguousarray(wd_full[2 * c:2 * c + 2])
        in_maps.append(m)
    res = run_bass_kernel_spmd(nc, in_maps, core_ids=list(range(8)))
    out = np.stack([np.asarray(res.results[c]["y"]) for c in range(8)]).astype(np.float32)
    if taps:
        kernel.last_results = res
    return out


# version 12
# speedup vs baseline: 1.0137x; 1.0137x over previous
import contextlib
import numpy as np
import ml_dtypes
import concourse.bass as bass
import concourse.mybir as mybir
from concourse.bass_utils import run_bass_kernel_spmd

F32 = mybir.dt.float32
BF16 = mybir.dt.bfloat16
U32 = mybir.dt.uint32
AF = mybir.ActivationFunctionType
ALU = mybir.AluOpType
AX = mybir.AxisListType

S, D = 4096, 1024
HEADS, HDIM = 16, 64
LEVELS, HID = 3, 4096
NEXP, TOPK, FF, CAP = 16, 2, 1408, 5120
NTOK = 512
NT = S // NTOK
NSUB = 4
EPS = 1e-5

DEBUG_TAPS = ()
BUILD_PHASES = 99


def build_program(taps=(), phases=99):
    nc = bass.Bass("TRN2", target_bir_lowering=False)

    def din(name, shape, dtype=F32):
        return nc.dram_tensor(name, shape, dtype, kind="ExternalInput").ap()

    x_h = din("x", [S, D])
    wq_h = din("wq", [D, D]); wk_h = din("wk", [D, D])
    wv_h = din("wv", [D, D]); wo_h = din("wo", [D, D])
    ln1g_h = din("ln1g", [D]); ln1b_h = din("ln1b", [D])
    ln2g_h = din("ln2g", [D]); ln2b_h = din("ln2b", [D])
    cw1_h = din("cw1", [LEVELS, D, HID]); cb1_h = din("cb1", [LEVELS, HID])
    cw2_h = din("cw2", [LEVELS, HID, D]); cb2_h = din("cb2", [LEVELS, D])
    rw_h = din("rw", [D, NEXP]); rb_h = din("rb", [NEXP])
    wg_h = din("wg", [NEXP, D, FF]); wu_h = din("wu", [NEXP, D, FF])
    wd_h = din("wd", [NEXP, FF, D])
    swg_h = din("swg", [D, FF]); swu_h = din("swu", [D, FF]); swd_h = din("swd", [FF, D])
    wgo_h = din("wgo", [2, D, FF]); wuo_h = din("wuo", [2, D, FF]); wdo_h = din("wdo", [2, FF, D])
    halfpar_h = din("halfpar", [1, 2 * NEXP])
    rankvec_h = din("rankvec_h", [128, 1])
    identb_h = din("identb_h", [128, 128], BF16)
    identf_h = din("identf_h", [128, 128], F32)
    maskt_h = din("maskt_h", [128, 128], F32)
    trius_h = din("trius_h", [128, 128], BF16)
    rankpre_h = din("rankpre_h", [8, 1], F32)

    y_h = nc.dram_tensor("y", [S, D], F32, kind="ExternalOutput").ap()

    def scrt(name, shape, dtype):
        kind = "ExternalOutput" if name in taps else "Internal"
        return nc.dram_tensor(name, shape, dtype, kind=kind).ap()

    sp_xa = scrt("sp_xa", [S, D], F32)
    sp_h = scrt("sp_h", [S, D], BF16)
    sp_hA = scrt("sp_hA", [S, D], BF16)
    sp_hB = scrt("sp_hB", [S, D], BF16)
    sp_h3 = scrt("sp_h3", [S, D], F32)
    sp_x2 = scrt("sp_x2", [S, D], F32)
    sp_z2 = scrt("sp_z2", [S, D], F32)
    sp_z2fm = scrt("sp_z2fm", [D, S], BF16)
    sp_acc = scrt("sp_acc", [S, D], BF16)
    sp_z2tm = scrt("sp_z2tm", [S, D], BF16)
    z2_all = nc.dram_tensor("z2_all", [8 * S, D], BF16, kind="Internal").ap()
    t_c = nc.dram_tensor("t_c", [16 * 128 * 40, 1], F32, kind="Internal").ap()
    t_own = nc.dram_tensor("t_own", [2 * 128 * 40, 1], F32, kind="Internal").ap()
    outA = nc.dram_tensor("outA", [CAP, D], BF16, kind="Internal").ap()
    outB = nc.dram_tensor("outB", [CAP, D], BF16, kind="Internal").ap()
    all2 = nc.dram_tensor("all2", [16 * CAP + 128, D], BF16, kind="Internal").ap()
    cc_in = nc.dram_tensor("cc_in", [1, NEXP], F32, kind="Internal").ap()
    cc_out = nc.dram_tensor("cc_out", [8, NEXP], F32, kind="Internal").ap()
    tap_small = scrt("tap_small", [8, 64], F32)

    with contextlib.ExitStack() as st_:
        sem = {}
        cnt = {}
        for sname in ("qs", "qp", "qa", "qw", "pe", "act", "dve", "pool", "cc"):
            sem[sname] = st_.enter_context(nc.semaphore(sname))
            cnt[sname] = 0

        def inc(inst, s, k=1):
            cnt[s] += k
            inst.then_inc(sem[s], k)
            return (s, cnt[s])

        def now(s):
            return (s, cnt[s])

        def wge(eng, *evs):
            for ev in evs:
                if ev is not None and ev[1] > 0:
                    if ev[0] in ("qs", "qp", "qa", "qw"):
                        eng.wait_ge(sem[ev[0]], cnt[ev[0]])
                    else:
                        eng.wait_ge(sem[ev[0]], ev[1])

        _uid = [0]

        def _un(name):
            _uid[0] += 1
            return f"{name}_{_uid[0]}"

        sb = lambda name, shape, dt=F32: st_.enter_context(nc.sbuf_tensor(name, shape, dt))
        identb = sb("identb", [128, 128], BF16)
        identf = sb("identf", [128, 128], F32)
        maskt = sb("maskt", [128, 128], F32)
        trius = sb("trius", [128, 128], BF16)
        rankpre = sb("rankpre", [8, 1], F32)
        consts = sb("consts", [128, 128], F32)
        b2bc = sb("b2bc", [128, LEVELS * D], BF16)
        rbbc = sb("rbbc", [128, NEXP], F32)
        iota16 = sb("iota16", [128, NEXP], F32)
        epsv = sb("epsv", [128, 1], F32)
        onesf = sb("onesf", [1, 128], F32)
        onesbc = sb("onesbc", [128, 1], BF16)
        onesgc = sb("onesgc", [128, 1], F32)
        zerosb = sb("zerosb", [128, 512], BF16)
        rw_sb = sb("rw_sb", [128, 8 * NEXP], BF16)
        oh1s = sb("oh1s", [128, 32 * NEXP], BF16)
        oh2s = sb("oh2s", [128, 32 * NEXP], BF16)
        ohsumb = sb("ohsumb", [128, 32 * NEXP], BF16)
        gstore = sb("gstore", [128, 64], F32)
        gwst = sb("gwst", [128, 64], F32)
        halfbc = sb("halfbc", [128, NEXP], F32)
        parbc = sb("parbc", [128, NEXP], F32)
        iotacol = sb("iotacol", [128, 1], F32)
        oobcol = sb("oobcol", [128, 1], F32)
        zrowcol = sb("zrowcol", [128, 1], F32)
        rankvec = sb("rankvec", [128, 1], F32)
        sidxu = sb("sidxu", [128, 64], U32)
        cidxu = sb("cidxu", [128, 64], U32)
        tokidu = sb("tokidu", [128, 32], U32)
        cnts_all = sb("cnts_all", [8, NEXP], F32)
        offs = sb("offs", [1, NEXP], F32)
        base = sb("base", [1, NEXP], F32)
        cntrow = sb("cntrow", [1, NEXP], F32)
        scr16 = sb("scr16", [128, 10 * NEXP], F32)
        pos1 = sb("pos1", [128, 1], F32)
        pos2 = sb("pos2", [128, 1], F32)
        bv1 = sb("bv1", [128, 1], F32)
        bv2 = sb("bv2", [128, 1], F32)
        w1c = sb("w1c", [128, 1], F32)
        w2c = sb("w2c", [128, 1], F32)
        aux = sb("aux", [128, 10], F32)
        auxu = sb("auxu", [128, 2], U32)
        bnst = sb("bnst", [128, 96], F32)
        mstat = sb("mstat", [128, 16], F32)
        rsq = sb("rsq", [128, 8], F32)
        rsi = sb("rsi", [128, 8], F32)
        nmb = sb("nmb", [128, 8], F32)

        with nc.Block() as block:
            def sec(engname, f):
                getattr(block, engname)(f)

            def phase_barrier():
                marks = [now(k) for k in ("qs", "qp", "pe", "act", "dve", "pool")]
                for engname in ("sync", "gpsimd", "vector", "scalar", "tensor"):
                    def bwait(eng, marks=marks):
                        wge(eng, *marks)
                    sec(engname, bwait)

            with contextlib.ExitStack() as p01:
                w01 = lambda name: p01.enter_context(nc.sbuf_tensor(_un(name), [128, 8 * D], BF16))
                wq_sb = w01("wq_sb"); wk_sb = w01("wk_sb")
                wv_sb = w01("wv_sb"); wo_sb = w01("wo_sb")

                with contextlib.ExitStack() as prep_st:
                    psb = lambda name, shape: prep_st.enter_context(nc.sbuf_tensor(_un(name), shape, F32))
                    lnrows = psb("lnrows", [8, 512])
                    hprow = psb("hprow", [1, 2 * NEXP])
                    b1rows = psb("b1rows", [32, LEVELS * 128])
                    b2row = psb("b2row", [1, LEVELS * D])
                    rbrow = psb("rbrow", [1, NEXP])
                    with contextlib.ExitStack() as pst:
                        pP = pst.enter_context(nc.psum_tensor(_un("pP"), [128, 512], F32))
                        pQ0 = pst.enter_context(nc.psum_tensor(_un("pQ0"), [128, 512], F32))
                        pQ1 = pst.enter_context(nc.psum_tensor(_un("pQ1"), [128, 512], F32))

                        def g_loads(g):
                            g.iota(iotacol[:], pattern=[[1, 1]], base=0, channel_multiplier=1,
                                   allow_small_or_imprecise_dtypes=True)
                            for wsb, wh in ((wq_sb, wq_h), (wk_sb, wk_h), (wv_sb, wv_h), (wo_sb, wo_h)):
                                inc(g.dma_start(out=wsb[:].rearrange("p (i j) -> p i j", j=D),
                                                in_=wh.rearrange("(i p) j -> p i j", p=128)), "qp", 16)
                            inc(g.dma_start(out=rw_sb[:].rearrange("p (i j) -> p i j", j=NEXP),
                                            in_=rw_h.rearrange("(i p) j -> p i j", p=128)), "qp", 16)
                            iv = g.iota(iota16[:], pattern=[[1, NEXP]], base=0, channel_multiplier=0,
                                        allow_small_or_imprecise_dtypes=True)
                            inc(iv, "pool")
                        sec("gpsimd", g_loads)
                        ev_qp_prep = now("qp")

                        def s_loads(s):
                            inc(s.dma_start(out=identb[:], in_=identb_h[:, :]), "qs", 16)
                            inc(s.dma_start(out=identf[:], in_=identf_h[:, :]), "qs", 16)
                            inc(s.dma_start(out=maskt[:], in_=maskt_h[:, :]), "qs", 16)
                            inc(s.dma_start(out=trius[:], in_=trius_h[:, :]), "qs", 16)
                            inc(s.dma_start(out=rankpre[:], in_=rankpre_h[:, :]), "qs", 16)
                            for i, hsrc in enumerate((ln1g_h, ln1b_h, ln2g_h, ln2b_h)):
                                inc(s.dma_start(out=lnrows[0:8, i * 128:(i + 1) * 128],
                                                in_=hsrc.rearrange("(q p) -> q p", p=128)), "qs", 16)
                            for l in range(LEVELS):
                                inc(s.dma_start(out=b1rows[0:32, l * 128:(l + 1) * 128],
                                                in_=cb1_h[l, :].rearrange("(q p) -> q p", p=128)), "qs", 16)
                            inc(s.dma_start(out=b2row[0:1, :],
                                            in_=cb2_h.rearrange("l j -> (l j)").rearrange("(a k) -> a k", a=1)),
                                "qs", 16)
                            inc(s.dma_start(out=rbrow[0:1, :],
                                            in_=rb_h.rearrange("(a k) -> a k", a=1)), "qs", 16)
                            inc(s.dma_start(out=hprow[0:1, :], in_=halfpar_h[:, :]), "qs", 16)
                            inc(s.dma_start(out=rankvec[:], in_=rankvec_h[:, :]), "qs", 16)
                        sec("sync", s_loads)
                        ev_qs_prep = now("qs")

                        def v_memsets(v):
                            wge(v, now("pool"))
                            v.tensor_scalar(out=oobcol[:], in0=iotacol[:], scalar1=1.0,
                                            op0=ALU.mult, scalar2=131072.0, op1=ALU.add)
                            v.tensor_scalar(out=zrowcol[:], in0=iotacol[:], scalar1=1.0,
                                            op0=ALU.mult, scalar2=float(16 * CAP), op1=ALU.add)
                            v.memset(epsv[:], EPS)
                            v.memset(onesf[:], 1.0)
                            v.memset(onesbc[:], 1.0)
                            v.memset(onesgc[:], 1.0)
                            v.memset(zerosb[:], 0.0)
                            iv = v.memset(base[:], 0.0)
                            inc(iv, "dve")
                        sec("vector", v_memsets)
                        ev_memset = now("dve")

                        def t_prep(t):
                            wge(t, ev_qs_prep)
                            for i in range(4):
                                t.transpose(pP[:, i * 8:(i + 1) * 8],
                                            lnrows[0:8, i * 128:(i + 1) * 128], identf[0:8, 0:8])
                            for l in range(LEVELS):
                                iv = t.transpose(pP[:, 32 + l * 32:64 + l * 32],
                                                 b1rows[0:32, l * 128:(l + 1) * 128], identf[0:32, 0:32])
                            inc(iv, "pe")
                        sec("tensor", t_prep)
                        ev_pe_prep = now("pe")

                        def v_prep(v):
                            wge(v, ev_pe_prep)
                            iv = v.tensor_copy(consts[:], pP[:, 0:128])
                            inc(iv, "dve")
                        sec("vector", v_prep)
                        ev_consts = now("dve")

                        bc_ev = {}
                        dve_base_bc = cnt["dve"]

                        def t_bc(t):
                            wge(t, ev_consts)
                            for i in range(9):
                                bank = pQ0 if i % 2 == 0 else pQ1
                                if i >= 2:
                                    wge(t, ("dve", dve_base_bc + i - 1))
                                if i < 6:
                                    iv = t.matmul(bank[:], onesf[:], b2row[0:1, i * 512:(i + 1) * 512],
                                                  start=True, stop=True)
                                elif i == 6:
                                    iv = t.matmul(bank[:, 0:NEXP], onesf[:], rbrow[0:1, :],
                                                  start=True, stop=True)
                                else:
                                    iv = t.matmul(bank[:, 0:NEXP], onesf[:],
                                                  hprow[0:1, (i - 7) * NEXP:(i - 6) * NEXP],
                                                  start=True, stop=True)
                                bc_ev[("m", i)] = inc(iv, "pe")
                        sec("tensor", t_bc)

                        def v_bc(v):
                            for i in range(9):
                                bank = pQ0 if i % 2 == 0 else pQ1
                                wge(v, bc_ev[("m", i)])
                                if i < 6:
                                    iv = v.tensor_copy(b2bc[:, i * 512:(i + 1) * 512], bank[:, 0:512])
                                elif i == 6:
                                    iv = v.tensor_copy(rbbc[:], bank[:, 0:NEXP])
                                elif i == 7:
                                    iv = v.tensor_copy(halfbc[:], bank[:, 0:NEXP])
                                else:
                                    iv = v.tensor_copy(parbc[:], bank[:, 0:NEXP])
                                inc(iv, "dve")
                        sec("vector", v_bc)
                        assert cnt["dve"] == dve_base_bc + 9
                        prep_end = {k: now(k) for k in sem}

                phase_barrier()
                with contextlib.ExitStack() as p1:
                    a = lambda name, shape, dt: p1.enter_context(nc.sbuf_tensor(_un(name), shape, dt))
                    xt = a("xt", [128, NSUB * D], F32)
                    z1 = [a("z1", [128, NSUB * D], F32) for _ in range(2)]
                    z1n = a("z1n", [128, NSUB * D], BF16)
                    h0 = a("h0", [128, NSUB * D], BF16)
                    qf = a("qf", [128, NSUB * D], BF16)
                    kf = a("kf", [128, NSUB * D], BF16)
                    vt = a("vt", [128, NSUB * D], BF16)
                    kt = a("kt", [128, NSUB * D], BF16)
                    yf = a("yf", [128, NSUB * D], BF16)
                    xa = a("xa", [128, NSUB * D], F32)
                    hno = yf
                    mts = a("mts", [128, 512], BF16)
                    mtsf = a("mtsf", [128, 512], F32)
                    scs0 = a("scs0", [128, 128], BF16)
                    scs1 = a("scs1", [128, 128], BF16)
                    with contextlib.ExitStack() as pst:
                        ps = lambda name, shape, dt=F32: pst.enter_context(nc.psum_tensor(_un(name), shape, dt))
                        pDM = ps("pDM", [128, 512])
                        pT0 = ps("pT0", [128, 1024], BF16)
                        pT1 = ps("pT1", [128, 1024], BF16)
                        pA = ps("pA", [128, 512])
                        pB = ps("pB", [128, 512])
                        pS0 = ps("pS0", [128, 128])
                        pS1 = ps("pS1", [128, 128])

                        def ln_norm(src, dst, ev_ready, war_ev=None, act_out=False):
                            evd = {}

                            def v1(v):
                                wge(v, ev_ready)
                                for s in range(NSUB):
                                    c0 = s * D
                                    v.bn_stats(bnst[:, s * 12:s * 12 + 6], src[:, c0:c0 + 512])
                                    ev = inc(v.bn_stats(bnst[:, s * 12 + 6:s * 12 + 12],
                                                        src[:, c0 + 512:c0 + D]), "dve")
                                    wge(v, ev)
                                    ev = inc(v.bn_aggr(mstat[:, 2 * s:2 * s + 2],
                                                       bnst[:, s * 12:s * 12 + 12]), "dve")
                                    evd[("agg", s)] = ev
                            sec("vector", v1)

                            def s1(sc):
                                for s in range(NSUB):
                                    wge(sc, evd[("agg", s)])
                                    iv = sc.activation(out=rsq[:, s:s + 1], in_=mstat[:, 2 * s + 1:2 * s + 2],
                                                       func=AF.Sqrt, bias=epsv[:])
                                    evd[("sq", s)] = inc(iv, "act")
                            sec("scalar", s1)

                            if act_out:
                                def v2s(v):
                                    for s in range(NSUB):
                                        wge(v, evd[("sq", s)])
                                        ev = inc(v.reciprocal(rsi[:, s:s + 1], rsq[:, s:s + 1]), "dve")
                                        wge(v, ev)
                                        iv = v.tensor_scalar(out=nmb[:, s:s + 1],
                                                             in0=mstat[:, 2 * s:2 * s + 1],
                                                             scalar1=rsi[:, s:s + 1], op0=ALU.mult,
                                                             scalar2=-1.0, op1=ALU.mult)
                                        evd[("nm", s)] = inc(iv, "dve")
                                sec("vector", v2s)

                                def s2(sc):
                                    if war_ev is not None:
                                        wge(sc, war_ev)
                                    for s in range(NSUB):
                                        wge(sc, evd[("nm", s)])
                                        c0 = s * D
                                        for half in range(2):
                                            iv = sc.activation(
                                                out=dst[:, c0 + half * 512:c0 + half * 512 + 512],
                                                in_=src[:, c0 + half * 512:c0 + half * 512 + 512],
                                                func=AF.Identity, scale=rsi[:, s:s + 1],
                                                bias=nmb[:, s:s + 1])
                                            inc(iv, "act")
                                sec("scalar", s2)
                                return now("act")

                            def v2(v):
                                if war_ev is not None:
                                    wge(v, war_ev)
                                for s in range(NSUB):
                                    wge(v, evd[("sq", s)])
                                    ev = inc(v.reciprocal(rsi[:, s:s + 1], rsq[:, s:s + 1]), "dve")
                                    wge(v, ev)
                                    c0 = s * D
                                    iv = v.tensor_scalar(out=dst[:, c0:c0 + D], in0=src[:, c0:c0 + D],
                                                         scalar1=mstat[:, 2 * s:2 * s + 1], op0=ALU.subtract,
                                                         scalar2=rsi[:, s:s + 1], op1=ALU.mult)
                                    inc(iv, "dve")
                            sec("vector", v2)
                            return now("dve")

                        def v_init(v):
                            v.memset(mts[:], 0.0)
                            inc(v.memset(mtsf[:], 0.0), "dve")
                        sec("vector", v_init)
                        ev_mts0 = now("dve")

                        ev_xt_free = [None]
                        ev_z1n_war = [None]
                        ev_z1n_done = [None]
                        ev_spill_xa = [None] * NT
                        ev_spill_h = [None] * NT
                        attn_gate = {}
                        prev_wo_tail = None

                        def emit_front(tau):
                            par = tau % 2
                            hold = {}

                            def s_load(sy, tau=tau, free=ev_xt_free[0]):
                                if tau == 0:
                                    wge(sy, prep_end["qs"])
                                wge(sy, free)
                                wge(sy, ("qs", 1))
                                iv = sy.dma_start(
                                    out=xt[:].rearrange("p (s j) -> p s j", j=D),
                                    in_=x_h[tau * NTOK:(tau + 1) * NTOK, :].rearrange("(s p) j -> p s j", p=128))
                                hold["x"] = inc(iv, "qs", 16)
                            sec("sync", s_load)

                            ev_z1 = ln_norm(xt, z1[par], hold["x"], act_out=True)
                            ev_xt_free[0] = ev_z1
                            ln_norm(z1[par], z1n, ev_z1, war_ev=ev_z1n_war[0], act_out=True)
                            ev_z1n_done[0] = now("act")

                        emit_front(0)
                        for tau in range(NT):
                            par = tau % 2
                            act_base = cnt["act"]
                            tr_ev = {}

                            def t_tr(t, tau=tau, ev_z1n=ev_z1n_done[0], act_base=act_base):
                                wge(t, ev_z1n)
                                for ic in range(8):
                                    bank = pT0 if ic % 2 == 0 else pT1
                                    if ic >= 2:
                                        wge(t, ("act", act_base + ic - 1))
                                    for s in range(NSUB):
                                        iv = t.transpose(bank[:, s * 128:(s + 1) * 128],
                                                         z1n[:, s * D + ic * 128:s * D + ic * 128 + 128],
                                                         identb[:])
                                    tr_ev[("t", ic)] = inc(iv, "pe")
                            sec("tensor", t_tr)
                            ev_z1n_war[0] = now("pe")

                            def s_tr(sc, tau=tau):
                                for ic in range(8):
                                    bank = pT0 if ic % 2 == 0 else pT1
                                    wge(sc, tr_ev[("t", ic)])
                                    iv = sc.activation(out=h0[:, ic * 512:(ic + 1) * 512],
                                                       in_=bank[:, 0:512], func=AF.Identity,
                                                       scale=consts[:, ic:ic + 1],
                                                       bias=consts[:, 8 + ic:9 + ic])
                                    inc(iv, "act")
                            sec("scalar", s_tr)
                            assert cnt["act"] == act_base + 8
                            ev_h0 = now("act")

                            act_base = cnt["act"]
                            qkv_ev = {}

                            def t_qkv(t, tau=tau, ev_h0=ev_h0, act_base=act_base,
                                      wo_tail=prev_wo_tail):
                                wge(t, ev_h0)
                                wge(t, wo_tail)
                                if tau == 0:
                                    wge(t, prep_end["qp"])
                                pb = 0
                                for wsb in (wq_sb, wk_sb):
                                    for oc in range(8):
                                        bank = pA if pb % 2 == 0 else pB
                                        if pb >= 2:
                                            wge(t, ("act", act_base + pb - 1))
                                        for ic in range(8):
                                            iv = t.matmul(bank[:],
                                                          wsb[:, ic * D + oc * 128:ic * D + oc * 128 + 128],
                                                          h0[:, ic * 512:(ic + 1) * 512],
                                                          start=(ic == 0), stop=(ic == 7))
                                        qkv_ev[("m", pb)] = inc(iv, "pe")
                                        pb += 1
                                for s in range(NSUB):
                                    for half in range(2):
                                        bank = pA if pb % 2 == 0 else pB
                                        wge(t, ("act", act_base + pb - 1))
                                        for ic in range(8):
                                            iv = t.matmul(bank[:],
                                                          h0[:, ic * 512 + s * 128:ic * 512 + s * 128 + 128],
                                                          wv_sb[:, ic * D + half * 512:ic * D + half * 512 + 512],
                                                          start=(ic == 0), stop=(ic == 7))
                                        qkv_ev[("m", pb)] = inc(iv, "pe")
                                        pb += 1
                            sec("tensor", t_qkv)

                            def s_qkv(sc, tau=tau):
                                pb = 0
                                for dest in (qf, kf):
                                    for oc in range(8):
                                        bank = pA if pb % 2 == 0 else pB
                                        wge(sc, qkv_ev[("m", pb)])
                                        iv = sc.copy(out=dest[:, oc * 512:(oc + 1) * 512], in_=bank[:])
                                        inc(iv, "act")
                                        pb += 1
                                for s in range(NSUB):
                                    for half in range(2):
                                        bank = pA if pb % 2 == 0 else pB
                                        wge(sc, qkv_ev[("m", pb)])
                                        iv = sc.copy(out=vt[:, s * D + half * 512:s * D + half * 512 + 512],
                                                     in_=bank[:])
                                        inc(iv, "act")
                                        pb += 1
                            sec("scalar", s_qkv)
                            assert cnt["act"] == act_base + 24
                            ev_qkv = now("act")

                            dve_base = cnt["dve"]
                            kt_ev = {}

                            def t_kt(t, tau=tau, ev_qkv=ev_qkv, dve_base=dve_base):
                                wge(t, ev_qkv)
                                for s in range(NSUB):
                                    bank = pT0 if s % 2 == 0 else pT1
                                    if s >= 2:
                                        wge(t, ("dve", dve_base + s - 1))
                                    for ic in range(8):
                                        iv = t.transpose(bank[:, ic * 128:(ic + 1) * 128],
                                                         kf[:, ic * 512 + s * 128:ic * 512 + s * 128 + 128],
                                                         identb[:])
                                    kt_ev[("t", s)] = inc(iv, "pe")
                            sec("tensor", t_kt)

                            def v_kt(v, tau=tau):
                                for s in range(NSUB):
                                    bank = pT0 if s % 2 == 0 else pT1
                                    wge(v, kt_ev[("t", s)])
                                    iv = v.tensor_copy(kt[:, s * D:(s + 1) * D], bank[:])
                                    inc(iv, "dve")
                            sec("vector", v_kt)
                            assert cnt["dve"] == dve_base + 4
                            ev_kt = now("dve")

                            for s in range(NSUB):
                                pe_base = cnt["pe"]
                                dve_base = cnt["dve"]
                                sc_ev = [("pe", pe_base + i + 1) for i in range(HEADS)]
                                msk_ev = [("dve", dve_base + i + 1) for i in range(HEADS)]

                                def t_at(t, tau=tau, s=s, ev_kt=ev_kt, sc_ev=sc_ev, msk_ev=msk_ev,
                                         gate=dict(attn_gate)):
                                    wge(t, ev_kt)
                                    if tau == 0 and s == 0:
                                        wge(t, prep_end["dve"], ev_mts0)
                                    else:
                                        wge(t, gate["mtcopy"], gate["ytevac"])
                                    for h in range(HEADS):
                                        p0 = (h % 2) * 64
                                        oc = h // 2
                                        sbank = pS0 if h % 2 == 0 else pS1
                                        if h >= 2:
                                            wge(t, msk_ev[h - 2])
                                        iv = t.matmul(
                                            sbank[:],
                                            kf[p0:p0 + 64, oc * 512 + s * 128:oc * 512 + s * 128 + 128],
                                            qf[p0:p0 + 64, oc * 512 + s * 128:oc * 512 + s * 128 + 128],
                                            start=True, stop=True)
                                        inc(iv, "pe")
                                        t.matmul(pDM[p0:p0 + 64, oc * 64:oc * 64 + 64],
                                                 kt[:, s * D + h * 64:s * D + h * 64 + 64],
                                                 vt[:, s * D + h * 64:s * D + h * 64 + 64],
                                                 start=(h < 2), stop=(h >= HEADS - 2),
                                                 skip_group_check=True)
                                        ybank = pA if h < 8 else pB
                                        ycol = (oc % 4) * 128
                                        wge(t, msk_ev[h])
                                        t.matmul(ybank[p0:p0 + 64, ycol:ycol + 128],
                                                 mts[p0:p0 + 64, oc * 64:oc * 64 + 64],
                                                 qf[p0:p0 + 64, oc * 512 + s * 128:oc * 512 + s * 128 + 128],
                                                 start=True, stop=False, skip_group_check=True)
                                        scsrc = scs0 if h % 2 == 0 else scs1
                                        iv = t.matmul(ybank[p0:p0 + 64, ycol:ycol + 128],
                                                      vt[:, s * D + h * 64:s * D + h * 64 + 64],
                                                      scsrc[:],
                                                      start=False, stop=True, skip_group_check=True)
                                    inc(iv, "pe")
                                sec("tensor", t_at)
                                assert cnt["pe"] == pe_base + HEADS + 1
                                pe_all = ("pe", pe_base + HEADS + 1)

                                def v_at(v, tau=tau, s=s, sc_ev=sc_ev, pe_all=pe_all):
                                    if s == 0 and tau > 0:
                                        wge(v, ev_spill_h[tau - 1])
                                    for h in range(HEADS):
                                        sbank = pS0 if h % 2 == 0 else pS1
                                        scdst = scs0 if h % 2 == 0 else scs1
                                        wge(v, sc_ev[h])
                                        iv = v.tensor_tensor(out=scdst[:], in0=sbank[:], in1=maskt[:],
                                                             op=ALU.mult)
                                        inc(iv, "dve")
                                    wge(v, pe_all)
                                    iv = v.tensor_copy(
                                        yf[:].rearrange("p (oc t) -> p oc t", t=512)[:, 0:4, s * 128:s * 128 + 128],
                                        pA[:].rearrange("p (oc t) -> p oc t", t=128))
                                    inc(iv, "dve")
                                    iv = v.tensor_copy(
                                        yf[:].rearrange("p (oc t) -> p oc t", t=512)[:, 4:8, s * 128:s * 128 + 128],
                                        pB[:].rearrange("p (oc t) -> p oc t", t=128))
                                    attn_gate["ytevac"] = inc(iv, "dve")
                                    iv = v.tensor_tensor(out=mtsf[:], in0=mtsf[:], in1=pDM[:],
                                                         op=ALU.add)
                                    attn_gate["mtadd"] = inc(iv, "dve")
                                sec("vector", v_at)
                                assert cnt["dve"] == dve_base + HEADS + 3

                                def sc_at(sc, s=s):
                                    wge(sc, attn_gate["mtadd"])
                                    iv = sc.copy(out=mts[:], in_=mtsf[:])
                                    attn_gate["mtcopy"] = inc(iv, "act")
                                sec("scalar", sc_at)

                            ev_attn = now("dve")

                            if tau + 1 < NT:
                                emit_front(tau + 1)

                            dve_base = cnt["dve"]
                            wo_ev = {}

                            def t_wo(t, tau=tau, ev_attn=ev_attn, dve_base=dve_base):
                                wge(t, ev_attn)
                                pb = 0
                                for s in range(NSUB):
                                    for half in range(2):
                                        bank = pA if pb % 2 == 0 else pB
                                        if pb >= 2:
                                            wge(t, ("dve", dve_base + pb - 1))
                                        for oc in range(8):
                                            iv = t.matmul(bank[:],
                                                          yf[:, oc * 512 + s * 128:oc * 512 + s * 128 + 128],
                                                          wo_sb[:, oc * D + half * 512:oc * D + half * 512 + 512],
                                                          start=(oc == 0), stop=(oc == 7))
                                        wo_ev[("m", pb)] = inc(iv, "pe")
                                        pb += 1
                            sec("tensor", t_wo)

                            def v_wo(v, tau=tau):
                                pb = 0
                                for s in range(NSUB):
                                    for half in range(2):
                                        bank = pA if pb % 2 == 0 else pB
                                        wge(v, wo_ev[("m", pb)])
                                        if pb == 0 and tau > 0:
                                            wge(v, ev_spill_xa[tau - 1])
                                        c0 = s * D + half * 512
                                        iv = v.tensor_tensor(out=xa[:, c0:c0 + 512], in0=bank[:],
                                                             in1=z1[par][:, c0:c0 + 512], op=ALU.add)
                                        inc(iv, "dve")
                                        pb += 1
                            sec("vector", v_wo)
                            assert cnt["dve"] == dve_base + 8
                            ev_wo = now("dve")
                            prev_wo_tail = ev_wo

                            def s_spill_xa(s, tau=tau, ev_wo=ev_wo):
                                wge(s, ev_wo)
                                iv = s.dma_start(
                                    out=sp_xa[tau * NTOK:(tau + 1) * NTOK, :].rearrange("(s p) j -> p s j", p=128),
                                    in_=xa[:].rearrange("p (s j) -> p s j", j=D))
                                ev_spill_xa[tau] = inc(iv, "qs", 16)
                            sec("sync", s_spill_xa)

                            ev_hno = ln_norm(xa, hno, ev_wo,
                                             war_ev=ev_spill_h[tau - 1] if tau > 0 else None)

                            def s_spill_h(s, tau=tau, ev_hno=ev_hno):
                                wge(s, ev_hno)
                                iv = s.dma_start(
                                    out=sp_h[tau * NTOK:(tau + 1) * NTOK, :].rearrange("(s p) j -> p s j", p=128),
                                    in_=hno[:].rearrange("p (s j) -> p s j", j=D))
                                ev_spill_h[tau] = inc(iv, "qs", 16)
                            sec("sync", s_spill_h)

                        p1_end = {k: now(k) for k in sem}

            if phases >= 2:
              phase_barrier()
              with contextlib.ExitStack() as p2:
                a = lambda name, shape, dt: p2.enter_context(nc.sbuf_tensor(_un(name), shape, dt))
                w1_sb = a("w1_sb", [128, 8 * HID], BF16)
                w2_sb = a("w2_sb", [128, 32 * D], BF16)
                ttm = a("ttm", [128, NSUB * D], BF16)
                infm = a("infm", [128, NSUB * D], BF16)
                hid = a("hid", [128, 32 * 512], BF16)
                hf = a("hf", [128, D], F32)
                with contextlib.ExitStack() as pst:
                    ps = lambda name, shape, dt=F32: pst.enter_context(nc.psum_tensor(_un(name), shape, dt))
                    pT0 = ps("pT0", [128, 512], BF16)
                    pT1 = ps("pT1", [128, 512], BF16)
                    pA = ps("pA", [128, 512])
                    pB = ps("pB", [128, 512])

                    lvl_src = [sp_h, sp_hB, sp_hA]
                    lvl_dst = [sp_hB, sp_hA, None]
                    lvl_spill = [[None] * NT for _ in range(LEVELS)]
                    ev_pe_lvl = p1_end["pe"]
                    ev_up_lvl = p1_end["pe"]

                    for l in range(LEVELS):
                        def g_w(g, l=l, ev_pe_lvl=ev_pe_lvl, ev_up_lvl=ev_up_lvl):
                            wge(g, ev_up_lvl)
                            iv = g.dma_start(out=w1_sb[:].rearrange("p (i j) -> p i j", j=HID),
                                             in_=cw1_h[l].rearrange("(i p) j -> p i j", p=128))
                            inc(iv, "qp", 16)
                            wge(g, ev_pe_lvl)
                            iv = g.dma_start(out=w2_sb[:].rearrange("p (i j) -> p i j", j=D),
                                             in_=cw2_h[l].rearrange("(i p) j -> p i j", p=128))
                            inc(iv, "qp", 16)
                        sec("gpsimd", g_w)
                        ev_w = now("qp")

                        ev_ttm_free = None
                        prev_dn_tail = None
                        for tau in range(NT):
                            hold = {}

                            def s_ld(s, l=l, tau=tau, ev_ttm_free=ev_ttm_free):
                                if l == 0:
                                    wge(s, ev_spill_h[tau])
                                else:
                                    wge(s, lvl_spill[l - 1][tau])
                                wge(s, ev_ttm_free)
                                iv = s.dma_start(
                                    out=ttm[:].rearrange("p (s j) -> p s j", j=D),
                                    in_=lvl_src[l][tau * NTOK:(tau + 1) * NTOK, :].rearrange(
                                        "(s p) j -> p s j", p=128))
                                hold["ttm"] = inc(iv, "qs", 16)
                            sec("sync", s_ld)
                            ev_ttm = hold["ttm"]

                            act_base = cnt["act"]
                            tr_ev = {}

                            def t_tr(t, l=l, tau=tau, ev_ttm=ev_ttm, act_base=act_base):
                                wge(t, ev_ttm)
                                for ic in range(8):
                                    bank = pT0 if ic % 2 == 0 else pT1
                                    if ic >= 2:
                                        wge(t, ("act", act_base + ic - 1))
                                    for s in range(NSUB):
                                        iv = t.transpose(bank[:, s * 128:(s + 1) * 128],
                                                         ttm[:, s * D + ic * 128:s * D + ic * 128 + 128],
                                                         identb[:])
                                    tr_ev[("t", ic)] = inc(iv, "pe")
                            sec("tensor", t_tr)
                            ev_ttm_free = now("pe")

                            def s_tr(sc, l=l, tau=tau):
                                for ic in range(8):
                                    bank = pT0 if ic % 2 == 0 else pT1
                                    wge(sc, tr_ev[("t", ic)])
                                    if l == 0:
                                        iv = sc.activation(out=infm[:, ic * 512:(ic + 1) * 512],
                                                           in_=bank[:], func=AF.Identity,
                                                           scale=consts[:, 16 + ic:17 + ic],
                                                           bias=consts[:, 24 + ic:25 + ic])
                                    else:
                                        iv = sc.copy(out=infm[:, ic * 512:(ic + 1) * 512], in_=bank[:])
                                    inc(iv, "act")
                            sec("scalar", s_tr)
                            assert cnt["act"] == act_base + 8
                            ev_infm = now("act")

                            act_base = cnt["act"]
                            up_ev = {}

                            def t_up(t, l=l, tau=tau, ev_infm=ev_infm, ev_w=ev_w, act_base=act_base,
                                     dn_tail=prev_dn_tail):
                                wge(t, ev_infm, ev_w)
                                wge(t, dn_tail)
                                for oc in range(32):
                                    bank = pA if oc % 2 == 0 else pB
                                    if oc >= 2:
                                        wge(t, ("act", act_base + oc - 1))
                                    for ic in range(8):
                                        iv = t.matmul(bank[:],
                                                      w1_sb[:, ic * HID + oc * 128:ic * HID + oc * 128 + 128],
                                                      infm[:, ic * 512:(ic + 1) * 512],
                                                      start=(ic == 0), stop=(ic == 7))
                                    up_ev[("m", oc)] = inc(iv, "pe")
                            sec("tensor", t_up)

                            def s_up(sc, l=l, tau=tau):
                                for oc in range(32):
                                    bank = pA if oc % 2 == 0 else pB
                                    wge(sc, up_ev[("m", oc)])
                                    iv = sc.activation(out=hid[:, oc * 512:(oc + 1) * 512], in_=bank[:],
                                                       func=AF.Gelu_apprx_tanh,
                                                       bias=consts[:, 32 + l * 32 + oc:33 + l * 32 + oc])
                                    inc(iv, "act")
                            sec("scalar", s_up)
                            assert cnt["act"] == act_base + 32
                            ev_hid = now("act")
                            if tau == NT - 1:
                                ev_up_lvl_next = now("pe")

                            dve_base = cnt["dve"]
                            dn_ev = {}
                            hf_sp = {}
                            for s in range(NSUB):
                                def t_dn(t, l=l, tau=tau, s=s, ev_hid=ev_hid, dve_base=dve_base):
                                    if s == 0:
                                        wge(t, ev_hid)
                                    for half in range(2):
                                        pb = s * 2 + half
                                        bank = pA if pb % 2 == 0 else pB
                                        if pb >= 2:
                                            wge(t, ("dve", dve_base + pb - 1))
                                        for oc in range(32):
                                            iv = t.matmul(bank[:],
                                                          hid[:, oc * 512 + s * 128:oc * 512 + s * 128 + 128],
                                                          w2_sb[:, oc * D + half * 512:oc * D + half * 512 + 512],
                                                          start=(oc == 0), stop=(oc == 31))
                                        dn_ev[("m", pb)] = inc(iv, "pe")
                                sec("tensor", t_dn)

                                def v_dn(v, l=l, tau=tau, s=s):
                                    for half in range(2):
                                        pb = s * 2 + half
                                        bank = pA if pb % 2 == 0 else pB
                                        wge(v, dn_ev[("m", pb)])
                                        if l < 2:
                                            iv = v.tensor_tensor(
                                                out=ttm[:, s * D + half * 512:s * D + half * 512 + 512],
                                                in0=bank[:],
                                                in1=b2bc[:, l * D + half * 512:l * D + half * 512 + 512],
                                                op=ALU.add)
                                        else:
                                            if half == 0 and s >= 1:
                                                wge(v, hf_sp[s - 1])
                                            iv = v.tensor_tensor(
                                                out=hf[:, half * 512:half * 512 + 512],
                                                in0=bank[:],
                                                in1=b2bc[:, 2 * D + half * 512:2 * D + half * 512 + 512],
                                                op=ALU.add)
                                        inc(iv, "dve")
                                sec("vector", v_dn)

                                if l == 2:
                                    def s_sp3(sy, tau=tau, s=s):
                                        wge(sy, now("dve"))
                                        iv = sy.dma_start(
                                            out=sp_h3[tau * NTOK + s * 128:tau * NTOK + (s + 1) * 128, :],
                                            in_=hf[:])
                                        hf_sp[s] = inc(iv, "qs", 16)
                                    sec("sync", s_sp3)
                            assert cnt["dve"] == dve_base + 8
                            prev_dn_tail = now("dve")

                            if l < 2:
                                ev_out = now("dve")

                                def s_sp(s, l=l, tau=tau, ev_out=ev_out):
                                    wge(s, ev_out)
                                    iv = s.dma_start(
                                        out=lvl_dst[l][tau * NTOK:(tau + 1) * NTOK, :].rearrange(
                                            "(s p) j -> p s j", p=128),
                                        in_=ttm[:].rearrange("p (s j) -> p s j", j=D))
                                    hold["sp"] = inc(iv, "qs", 16)
                                sec("sync", s_sp)
                                lvl_spill[l][tau] = hold["sp"]
                                ev_ttm_free = hold["sp"]
                            else:
                                lvl_spill[l][tau] = now("qs")
                        ev_pe_lvl = now("pe")
                        ev_up_lvl = ev_up_lvl_next

                    p2_end = {k: now(k) for k in sem}

            if phases >= 2.5:
              phase_barrier()
              with contextlib.ExitStack() as p2b:
                a = lambda name, shape, dt: p2b.enter_context(nc.sbuf_tensor(_un(name), shape, dt))
                h3t = [a("h3t", [128, NSUB * D], F32) for _ in range(2)]
                xat = [a("xat", [128, NSUB * D], F32) for _ in range(2)]
                xt2 = [a("xt2", [128, NSUB * D], F32) for _ in range(2)]
                oatt = a("oatt", [128, NSUB * D], F32)
                zpst = a("zpst", [128, NSUB * D], F32)
                x2tt = a("x2tt", [128, NSUB * D], F32)
                z2tt = a("z2tt", [128, NSUB * D], F32)
                z2nb = a("z2nb", [128, NSUB * D], BF16)
                z2fm = a("z2fm", [128, NSUB * D], BF16)
                logit = a("logit", [128, NEXP], F32)
                expz = a("expz", [128, NEXP], F32)
                probs = a("probs", [128, NEXP], F32)
                top8v = a("top8v", [128, 8], F32)
                top8i = a("top8i", [128, 8], U32)
                idxf = a("idxf", [128, 2], F32)
                sm1 = a("sm1", [128, 1], F32)
                sm2 = a("sm2", [128, 1], F32)
                with contextlib.ExitStack() as pst:
                    ps = lambda name, shape, dt=F32: pst.enter_context(nc.psum_tensor(_un(name), shape, dt))
                    pT0 = ps("pT0", [128, 512], BF16)
                    pT1 = ps("pT1", [128, 512], BF16)
                    pR = ps("pR", [128, NEXP])
                    pCNT = ps("pCNT", [1, NEXP])
                    pBCB = ps("pBCB", [128, NEXP])

                    def ln0_sub(src, dst, ev_ready, war_ev=None, bf_dst=None, bf_war=None,
                                slot=0):
                        b0 = slot * 12
                        def v1(v):
                            wge(v, ev_ready)
                            v.bn_stats(bnst[:, b0:b0 + 6], src[:, 0:512])
                            ev = inc(v.bn_stats(bnst[:, b0 + 6:b0 + 12], src[:, 512:D]), "dve")
                            wge(v, ev)
                            inc(v.bn_aggr(mstat[:, 2 * slot:2 * slot + 2], bnst[:, b0:b0 + 12]), "dve")
                        sec("vector", v1)
                        eva = now("dve")

                        def s1(sc):
                            wge(sc, eva)
                            inc(sc.activation(out=rsq[:, slot:slot + 1],
                                              in_=mstat[:, 2 * slot + 1:2 * slot + 2],
                                              func=AF.Sqrt, bias=epsv[:]), "act")
                        sec("scalar", s1)
                        evs_ = now("act")

                        def v2(v):
                            wge(v, evs_, war_ev)
                            ev = inc(v.reciprocal(rsi[:, slot:slot + 1], rsq[:, slot:slot + 1]), "dve")
                            wge(v, ev)
                            iv = v.tensor_scalar(out=dst[:, 0:D], in0=src[:, 0:D],
                                                 scalar1=mstat[:, 2 * slot:2 * slot + 1], op0=ALU.subtract,
                                                 scalar2=rsi[:, slot:slot + 1], op1=ALU.mult)
                            ev = inc(iv, "dve")
                            if bf_dst is not None:
                                wge(v, ev, *(bf_war or ()))
                                inc(v.tensor_copy(bf_dst, dst[:, 0:D]), "dve")
                        sec("vector", v2)
                        return now("dve")

                    ev_ld_free = [None, None]
                    ev_z2nb_war = None
                    ev_z2tm_war = None
                    ev_spfm_prev = None
                    ev_pool_tail = None
                    ev_x2sp = None
                    ev_z2sp = None
                    rt_prev = {}
                    for tau in range(NT):
                        par2 = tau % 2
                        h3t_, xat_, xt_ = h3t[par2], xat[par2], xt2[par2]
                        r0 = tau * NTOK
                        hold = {}

                        def s_ld(sy, r0=r0, h3t_=h3t_, xat_=xat_, xt_=xt_,
                                 free=ev_ld_free[par2], first=(tau == 0)):
                            if first:
                                wge(sy, p2_end["qs"])
                            wge(sy, free)
                            inc(sy.dma_start(out=h3t_[:].rearrange("p (s j) -> p s j", j=D),
                                             in_=sp_h3[r0:r0 + NTOK, :].rearrange("(s p) j -> p s j", p=128)), "qs", 16)
                            inc(sy.dma_start(out=xat_[:].rearrange("p (s j) -> p s j", j=D),
                                             in_=sp_xa[r0:r0 + NTOK, :].rearrange("(s p) j -> p s j", p=128)), "qs", 16)
                            hold["ld"] = inc(sy.dma_start(out=xt_[:].rearrange("p (s j) -> p s j", j=D),
                                             in_=x_h[r0:r0 + NTOK, :].rearrange("(s p) j -> p s j", p=128)), "qs", 16)
                        sec("sync", s_ld)

                        ag1 = {}

                        def v_st1(v, h3t_=h3t_, xat_=xat_):
                            wge(v, hold["ld"])
                            for sx in range(NSUB):
                                c0 = sx * D
                                ev = inc(v.tensor_tensor(out=oatt[:, c0:c0 + D], in0=h3t_[:, c0:c0 + D],
                                                         in1=xat_[:, c0:c0 + D], op=ALU.add), "dve")
                            wge(v, ev)
                            for sx in range(NSUB):
                                c0 = sx * D
                                v.bn_stats(bnst[:, sx * 12:sx * 12 + 6], oatt[:, c0:c0 + 512])
                                ev = inc(v.bn_stats(bnst[:, sx * 12 + 6:sx * 12 + 12],
                                                    oatt[:, c0 + 512:c0 + D]), "dve")
                                wge(v, ev)
                                ag1[sx] = inc(v.bn_aggr(mstat[:, 2 * sx:2 * sx + 2],
                                                        bnst[:, sx * 12:sx * 12 + 12]), "dve")
                        sec("vector", v_st1)

                        def s_sq1(sc):
                            for sx in range(NSUB):
                                wge(sc, ag1[sx])
                                ag1[("q", sx)] = inc(sc.activation(out=rsq[:, sx:sx + 1],
                                                                   in_=mstat[:, 2 * sx + 1:2 * sx + 2],
                                                                   func=AF.Sqrt, bias=epsv[:]), "act")
                        sec("scalar", s_sq1)

                        def v_n1(v):
                            for sx in range(NSUB):
                                wge(v, ag1[("q", sx)])
                                ev = inc(v.reciprocal(rsi[:, sx:sx + 1], rsq[:, sx:sx + 1]), "dve")
                                wge(v, ev)
                                c0 = sx * D
                                inc(v.tensor_scalar(out=zpst[:, c0:c0 + D], in0=oatt[:, c0:c0 + D],
                                                    scalar1=mstat[:, 2 * sx:2 * sx + 1], op0=ALU.subtract,
                                                    scalar2=rsi[:, sx:sx + 1], op1=ALU.mult), "dve")
                        sec("vector", v_n1)
                        ev_zp = now("dve")

                        def p_x2(g, xt_=xt_, ev_zp=ev_zp, ev_x2sp=ev_x2sp):
                            wge(g, ev_zp, ev_x2sp)
                            for sx in range(NSUB):
                                c0 = sx * D
                                inc(g.tensor_add(out=x2tt[:, c0:c0 + D], in0=zpst[:, c0:c0 + D],
                                                 in1=xt_[:, c0:c0 + D]), "pool")
                        sec("gpsimd", p_x2)
                        ev_x2 = now("pool")
                        ev_pool_tail = ev_x2

                        def s_spx2(sy, r0=r0, ev_x2=ev_x2):
                            wge(sy, ev_x2)
                            hold["x2sp"] = inc(sy.dma_start(
                                out=sp_x2[r0:r0 + NTOK, :].rearrange("(s p) j -> p s j", p=128),
                                in_=x2tt[:].rearrange("p (s j) -> p s j", j=D)), "qs", 16)
                        sec("sync", s_spx2)
                        ev_x2sp = hold["x2sp"]

                        ag2 = {}

                        def v_st2(v, ev_x2=ev_x2):
                            wge(v, ev_x2)
                            for sx in range(NSUB):
                                c0 = sx * D
                                v.bn_stats(bnst[:, 48 + sx * 12:48 + sx * 12 + 6], x2tt[:, c0:c0 + 512])
                                ev = inc(v.bn_stats(bnst[:, 48 + sx * 12 + 6:48 + sx * 12 + 12],
                                                    x2tt[:, c0 + 512:c0 + D]), "dve")
                                wge(v, ev)
                                ag2[sx] = inc(v.bn_aggr(mstat[:, 8 + 2 * sx:8 + 2 * sx + 2],
                                                        bnst[:, 48 + sx * 12:48 + sx * 12 + 12]), "dve")
                        sec("vector", v_st2)

                        def s_sq2(sc):
                            for sx in range(NSUB):
                                wge(sc, ag2[sx])
                                ag2[("q", sx)] = inc(sc.activation(out=rsq[:, 4 + sx:5 + sx],
                                                                   in_=mstat[:, 9 + 2 * sx:10 + 2 * sx],
                                                                   func=AF.Sqrt, bias=epsv[:]), "act")
                        sec("scalar", s_sq2)

                        def v_n2(v, tau=tau, ev_z2sp=ev_z2sp):
                            if tau > 0:
                                wge(v, ev_z2nb_war, ev_z2tm_war, ev_z2sp)
                            for sx in range(NSUB):
                                wge(v, ag2[("q", sx)])
                                ev = inc(v.reciprocal(rsi[:, 4 + sx:5 + sx], rsq[:, 4 + sx:5 + sx]), "dve")
                                wge(v, ev)
                                c0 = sx * D
                                ev = inc(v.tensor_scalar(out=z2tt[:, c0:c0 + D], in0=x2tt[:, c0:c0 + D],
                                                         scalar1=mstat[:, 8 + 2 * sx:9 + 2 * sx],
                                                         op0=ALU.subtract,
                                                         scalar2=rsi[:, 4 + sx:5 + sx], op1=ALU.mult), "dve")
                                wge(v, ev)
                                inc(v.tensor_copy(z2nb[:, c0:c0 + D], z2tt[:, c0:c0 + D]), "dve")
                        sec("vector", v_n2)
                        ev_z2done = now("dve")

                        def s_spz2(sy, r0=r0, ev_z2done=ev_z2done):
                            wge(sy, ev_z2done)
                            hold["z2sp"] = inc(sy.dma_start(
                                out=sp_z2[r0:r0 + NTOK, :].rearrange("(s p) j -> p s j", p=128),
                                in_=z2tt[:].rearrange("p (s j) -> p s j", j=D)), "qs", 16)
                        sec("sync", s_spz2)
                        ev_z2sp = hold["z2sp"]
                        ev_ld_free[par2] = ev_z2done

                        ev_z2nb = now("dve")

                        def s_spz2tm(sy, tau=tau, ev_z2nb=ev_z2nb):
                            wge(sy, ev_z2nb)
                            iv = sy.dma_start(
                                out=sp_z2tm[tau * NTOK:(tau + 1) * NTOK, :].rearrange("(s p) j -> p s j", p=128),
                                in_=z2nb[:].rearrange("p (s j) -> p s j", j=D))
                            hold_tm[0] = inc(iv, "qs", 16)
                        hold_tm = [None]
                        sec("sync", s_spz2tm)
                        ev_z2tm_war = hold_tm[0]

                        act_base = cnt["act"]
                        tr_ev = {}

                        def t_tr(t, tau=tau, ev_z2nb=ev_z2nb, act_base=act_base):
                            wge(t, ev_z2nb)
                            for ic in range(8):
                                bank = pT0 if ic % 2 == 0 else pT1
                                if ic >= 2:
                                    wge(t, ("act", act_base + ic - 1))
                                for s in range(NSUB):
                                    iv = t.transpose(bank[:, s * 128:(s + 1) * 128],
                                                     z2nb[:, s * D + ic * 128:s * D + ic * 128 + 128],
                                                     identb[:])
                                tr_ev[("t", ic)] = inc(iv, "pe")
                        sec("tensor", t_tr)
                        ev_z2nb_war = now("pe")

                        def s_tr(sc, tau=tau, ev_spfm_prev=ev_spfm_prev):
                            wge(sc, ev_spfm_prev)
                            for ic in range(8):
                                bank = pT0 if ic % 2 == 0 else pT1
                                wge(sc, tr_ev[("t", ic)])
                                iv = sc.copy(out=z2fm[:, ic * 512:(ic + 1) * 512], in_=bank[:])
                                inc(iv, "act")
                        sec("scalar", s_tr)
                        assert cnt["act"] == act_base + 8
                        ev_z2fm = now("act")

                        def s_spfm(sy, tau=tau, ev_z2fm=ev_z2fm):
                            wge(sy, ev_z2fm)
                            iv = sy.dma_start(
                                out=sp_z2fm.rearrange("(i p) t -> p i t", p=128)[:, :, tau * NTOK:(tau + 1) * NTOK],
                                in_=z2fm[:].rearrange("p (i t) -> p i t", t=512))
                            hold2[0] = inc(iv, "qs", 16)
                        hold2 = [None]
                        sec("sync", s_spfm)
                        ev_spfm_prev = hold2[0]

                        for s in range(NSUB):
                            st = tau * NSUB + s
                            rt = {}

                            def t_rt(t, s=s, st=st, ev_z2fm=ev_z2fm, rt_prev=dict(rt_prev)):
                                wge(t, ev_z2fm)
                                if st == 0:
                                    wge(t, prep_end["qp"])
                                if rt_prev:
                                    wge(t, rt_prev["evac"])
                                for ic in range(8):
                                    iv = t.matmul(pR[:], z2fm[:, ic * 512 + s * 128:ic * 512 + s * 128 + 128],
                                                  rw_sb[:, ic * NEXP:(ic + 1) * NEXP],
                                                  start=(ic == 0), stop=(ic == 7))
                                rt["mm"] = inc(iv, "pe")
                            sec("tensor", t_rt)

                            def v_rt1(v, st=st):
                                wge(v, rt["mm"])
                                ev = inc(v.tensor_tensor(out=logit[:], in0=pR[:], in1=rbbc[:], op=ALU.add), "dve")
                                rt["evac"] = ev
                                wge(v, ev)
                                ev = inc(v.tensor_reduce(out=sm1[:], in_=logit[:], axis=AX.X, op=ALU.max), "dve")
                                wge(v, ev)
                                inc(v.tensor_scalar(out=sm1[:], in0=sm1[:], scalar1=-1.0, scalar2=None,
                                                    op0=ALU.mult), "dve")
                            sec("vector", v_rt1)
                            ev_nm = now("dve")

                            def s_rt(sc, ev_nm=ev_nm):
                                wge(sc, ev_nm)
                                inc(sc.activation(out=expz[:], in_=logit[:], func=AF.Exp, bias=sm1[:]), "act")
                            sec("scalar", s_rt)
                            ev_ex = now("act")

                            def v_rt2(v, st=st, ev_ex=ev_ex):
                                wge(v, ev_ex)
                                ev = inc(v.tensor_reduce(out=sm2[:], in_=expz[:], axis=AX.X, op=ALU.add), "dve")
                                wge(v, ev)
                                ev = inc(v.reciprocal(sm2[:], sm2[:]), "dve")
                                wge(v, ev)
                                ev = inc(v.tensor_scalar(out=probs[:], in0=expz[:], scalar1=sm2[:],
                                                         scalar2=None, op0=ALU.mult), "dve")
                                wge(v, ev)
                                ev = inc(v.max(top8v[:], probs[:]), "dve")
                                wge(v, ev)
                                ev = inc(v.max_index(top8i[:], top8v[:], probs[:]), "dve")
                                wge(v, ev)
                                ev = inc(v.tensor_copy(idxf[:], top8i[:, 0:2]), "dve")
                                wge(v, ev)
                                inc(v.tensor_scalar(out=oh1s[:, st * NEXP:(st + 1) * NEXP], in0=iota16[:],
                                                    scalar1=idxf[:, 0:1], scalar2=None,
                                                    op0=ALU.is_equal), "dve")
                                ev = inc(v.tensor_scalar(out=oh2s[:, st * NEXP:(st + 1) * NEXP], in0=iota16[:],
                                                         scalar1=idxf[:, 1:2], scalar2=None,
                                                         op0=ALU.is_equal), "dve")
                                wge(v, ev)
                                inc(v.tensor_tensor(out=ohsumb[:, st * NEXP:(st + 1) * NEXP],
                                                    in0=oh1s[:, st * NEXP:(st + 1) * NEXP],
                                                    in1=oh2s[:, st * NEXP:(st + 1) * NEXP],
                                                    op=ALU.add), "dve")
                                ev = inc(v.tensor_tensor(out=sm1[:], in0=top8v[:, 0:1], in1=top8v[:, 1:2],
                                                         op=ALU.add), "dve")
                                wge(v, ev)
                                ev = inc(v.tensor_scalar(out=sm1[:], in0=sm1[:], scalar1=1e-9, scalar2=None,
                                                         op0=ALU.add), "dve")
                                wge(v, ev)
                                ev = inc(v.reciprocal(sm1[:], sm1[:]), "dve")
                                wge(v, ev)
                                inc(v.tensor_tensor(out=gstore[:, st * 2:st * 2 + 1], in0=top8v[:, 0:1],
                                                    in1=sm1[:], op=ALU.mult), "dve")
                                ev = inc(v.tensor_tensor(out=gstore[:, st * 2 + 1:st * 2 + 2],
                                                         in0=top8v[:, 1:2], in1=sm1[:], op=ALU.mult), "dve")
                                rt["oh"] = ev
                            sec("vector", v_rt2)

                            def t_cnt(t, st=st):
                                wge(t, rt["oh"])
                                iv = t.matmul(pCNT[:], onesbc[:], ohsumb[:, st * NEXP:(st + 1) * NEXP],
                                              start=(st == 0), stop=(st == 31))
                                inc(iv, "pe")
                            sec("tensor", t_cnt)
                            rt_prev = rt

                    ev_cnt_done = now("pe")

                    def v_cnt(v):
                        wge(v, ev_cnt_done)
                        inc(v.tensor_copy(cntrow[:], pCNT[:]), "dve")
                    sec("vector", v_cnt)
                    ev_cntrow = now("dve")

                    def g_cc(g):
                        wge(g, ev_cntrow)
                        inc(g.dma_start(out=cc_in[:, :], in_=cntrow[:]), "qp", 16)
                        wge(g, now("qp"))
                        iv = g.collective_compute(
                            "AllGather", ALU.bypass,
                            replica_groups=[[0, 1, 2, 3, 4, 5, 6, 7]],
                            ins=[cc_in[:, :].opt()],
                            outs=[cc_out[:, :].opt()])
                        inc(iv, "cc")
                        wge(g, now("cc"))
                        inc(g.dma_start(out=cnts_all[:], in_=cc_out[:, :]), "qp", 16)
                    sec("gpsimd", g_cc)
                    ev_cnts = now("qp")

                    def t_off(t):
                        wge(t, ev_cnts, prep_end["qs"])
                        inc(t.matmul(pR[0:1, :], rankpre[:], cnts_all[:], start=True, stop=True), "pe")
                    sec("tensor", t_off)
                    ev_offmm = now("pe")

                    def v_off(v):
                        wge(v, ev_offmm)
                        ev = inc(v.tensor_copy(offs[:], pR[0:1, :]), "dve")
                        wge(v, ev)
                        inc(v.tensor_copy(base[:], offs[:]), "dve")
                    sec("vector", v_off)
                    ev_base0 = now("dve")

                    if "tap_small" in taps:
                        def s_tap(sy):
                            wge(sy, ev_base0)
                            inc(sy.dma_start(out=tap_small[0:1, 0:NEXP], in_=cntrow[:]), "qs", 16)
                            inc(sy.dma_start(out=tap_small[1:2, 0:NEXP], in_=offs[:]), "qs", 16)
                            inc(sy.dma_start(out=tap_small[2:3, 0:NEXP],
                                             in_=cnts_all[0:1, :]), "qs", 16)
                        sec("sync", s_tap)


                    p2b_end = {k: now(k) for k in sem}

            if phases >= 3:
              phase_barrier()
              with contextlib.ExitStack() as p3:
                a = lambda name, shape, dt: p3.enter_context(nc.sbuf_tensor(_un(name), shape, dt))
                wg3 = a("wg3", [128, 8 * FF], BF16)
                wu3 = a("wu3", [128, 8 * FF], BF16)
                wd3 = a("wd3", [128, 11 * D], BF16)
                zgt = a("zgt", [128, NSUB * D], BF16)
                zgf = a("zgf", [128, NSUB * D], BF16)
                hid3 = a("hid3", [128, 11 * 512], BF16)
                silu_s = a("silu_s", [128, 1024], BF16)
                orow = a("orow", [128, NSUB * D], BF16)
                tabf = a("tabf", [128, 80], F32)
                tabu = a("tabu", [128, 80], U32)
                ztile = a("ztile", [128, 160], F32)
                gth = [a("gth", [128, 2 * D], BF16) for _ in range(2)]
                accL = a("accL", [128, D], BF16)
                z2l = a("z2l", [128, D], F32)
                x2l = a("x2l", [128, D], F32)
                offn = a("offn", [128, D], F32)
                outs4 = a("outs4", [128, D], F32)
                with contextlib.ExitStack() as pst:
                    ps = lambda name, shape, dt=F32: pst.enter_context(nc.psum_tensor(_un(name), shape, dt))
                    pT0 = ps("pT0", [128, 512], BF16)
                    pT1 = ps("pT1", [128, 512], BF16)
                    pU = [ps(f"pU{i}", [128, 512]) for i in range(4)]
                    pD0 = ps("pD0", [128, 512])
                    pD1 = ps("pD1", [128, 512])

                    p3_start = {k: now(k) for k in sem}

                    def v_zt(v):
                        iv = v.memset(ztile[:], 0.0)
                        inc(iv, "dve")
                        iv = v.memset(gth[0][:], 0.0)
                        inc(iv, "dve")
                        iv = v.memset(gth[1][:], 0.0)
                        inc(iv, "dve")
                    sec("vector", v_zt)
                    ev_zt = now("dve")

                    bndregs = {}

                    def g_kick(g):
                        bndregs["tc"] = g.alloc_register("bnd_tc")
                        bndregs["z2"] = g.alloc_register("bnd_z2")
                        bndregs["al"] = g.alloc_register("bnd_al")
                        g.reg_mov(bndregs["tc"], 16 * 128 * 40 - 1)
                        g.reg_mov(bndregs["z2"], 8 * S - 1)
                        g.reg_mov(bndregs["al"], 16 * CAP + 128 - 1)
                        inc(g.dma_start(out=wg3[:].rearrange("p (i j) -> p i j", j=FF),
                                        in_=swg_h.rearrange("(i p) j -> p i j", p=128)), "qw", 16)
                        inc(g.dma_start(out=wu3[:].rearrange("p (i j) -> p i j", j=FF),
                                        in_=swu_h.rearrange("(i p) j -> p i j", p=128)), "qw", 16)
                        inc(g.dma_start(out=wd3[:].rearrange("p (i j) -> p i j", j=D),
                                        in_=swd_h.rearrange("(i p) j -> p i j", p=128)), "qw", 16)
                        wge(g, ev_zt)
                        for i in range(4):
                            inc(g.dma_start(
                                out=t_c[i * 20480:(i + 1) * 20480, :].rearrange("(p a) k -> p a k", p=128),
                                in_=ztile[:, 0:160].rearrange("p (a k) -> p a k", k=1)), "qp", 16)
                        inc(g.dma_start(out=all2[16 * CAP:16 * CAP + 128, 0:512], in_=zerosb[:]), "qp", 16)
                        inc(g.dma_start(out=all2[16 * CAP:16 * CAP + 128, 512:1024], in_=zerosb[:]), "qp", 16)
                        iv = g.collective_compute(
                            "AllGather", ALU.bypass,
                            replica_groups=[[0, 1, 2, 3, 4, 5, 6, 7]],
                            ins=[sp_z2tm[:, :].opt()], outs=[z2_all[:, :].opt()])
                        inc(iv, "cc")
                    sec("gpsimd", g_kick)
                    ev_tabf = None
                    ev_cc_z2 = ("cc", cnt["cc"] - 1)
                    ev_cc_rs = ("cc", cnt["cc"])

                    if True:
                        pR = pD0
                        pBCB = pD1
                        pCNT = pU[0]
                        ev_gate = None
                        for st in range(32):
                            g_ev = {}

                            def t_g(t, st=st, ev_gate=ev_gate):
                                wge(t, ev_gate)
                                iv = t.matmul(pR[:, 0:NEXP], trius[:], ohsumb[:, st * NEXP:(st + 1) * NEXP],
                                              start=True, stop=True)
                                g_ev["cum"] = inc(iv, "pe")
                                iv = t.matmul(pBCB[:, 0:NEXP], onesf[:], base[:], start=True, stop=True)
                                g_ev["bc"] = inc(iv, "pe")
                                iv = t.matmul(pCNT[0:1, 0:NEXP], onesbc[:], ohsumb[:, st * NEXP:(st + 1) * NEXP],
                                              start=True, stop=True)
                                g_ev["cnt"] = inc(iv, "pe")
                            sec("tensor", t_g)

                            def v_g(v, st=st):
                                wge(v, g_ev["cum"], g_ev["bc"])
                                inc(v.scalar_tensor_tensor(out=scr16[:, 0:NEXP], in0=pR[:, 0:NEXP], scalar=1.0,
                                                           in1=oh1s[:, st * NEXP:(st + 1) * NEXP],
                                                           op0=ALU.mult, op1=ALU.mult,
                                                           accum_out=pos1[:]), "dve")
                                inc(v.scalar_tensor_tensor(out=scr16[:, NEXP:2 * NEXP], in0=pR[:, 0:NEXP], scalar=1.0,
                                                           in1=oh2s[:, st * NEXP:(st + 1) * NEXP],
                                                           op0=ALU.mult, op1=ALU.mult,
                                                           accum_out=pos2[:]), "dve")
                                inc(v.scalar_tensor_tensor(out=scr16[:, 2 * NEXP:3 * NEXP], in0=pBCB[:, 0:NEXP], scalar=1.0,
                                                           in1=oh1s[:, st * NEXP:(st + 1) * NEXP],
                                                           op0=ALU.mult, op1=ALU.mult,
                                                           accum_out=bv1[:]), "dve")
                                ev = inc(v.scalar_tensor_tensor(out=scr16[:, 3 * NEXP:4 * NEXP], in0=pBCB[:, 0:NEXP], scalar=1.0,
                                                                in1=oh2s[:, st * NEXP:(st + 1) * NEXP],
                                                                op0=ALU.mult, op1=ALU.mult,
                                                                accum_out=bv2[:]), "dve")
                                wge(v, ev)
                                inc(v.tensor_tensor(out=pos1[:], in0=pos1[:], in1=bv1[:], op=ALU.add), "dve")
                                ev = inc(v.tensor_tensor(out=pos2[:], in0=pos2[:], in1=bv2[:], op=ALU.add), "dve")
                                wge(v, ev)
                                inc(v.tensor_scalar(out=w1c[:], in0=pos1[:], scalar1=float(CAP - 1),
                                                    scalar2=None, op0=ALU.is_le), "dve")
                                inc(v.tensor_scalar(out=w2c[:], in0=pos2[:], scalar1=float(CAP - 1),
                                                    scalar2=None, op0=ALU.is_le), "dve")
                                inc(v.tensor_scalar(out=bv1[:], in0=pos1[:], scalar1=float(CAP - 1),
                                                    scalar2=None, op0=ALU.min), "dve")
                                ev = inc(v.tensor_scalar(out=bv2[:], in0=pos2[:], scalar1=float(CAP - 1),
                                                         scalar2=None, op0=ALU.min), "dve")
                                wge(v, ev)
                                inc(v.tensor_tensor(out=gwst[:, st * 2:st * 2 + 1], in0=w1c[:],
                                                    in1=gstore[:, st * 2:st * 2 + 1], op=ALU.mult), "dve")
                                inc(v.tensor_tensor(out=gwst[:, st * 2 + 1:st * 2 + 2], in0=w2c[:],
                                                    in1=gstore[:, st * 2 + 1:st * 2 + 2], op=ALU.mult), "dve")
                                inc(v.scalar_tensor_tensor(out=scr16[:, 4 * NEXP:5 * NEXP], in0=halfbc[:], scalar=1.0,
                                                           in1=oh1s[:, st * NEXP:(st + 1) * NEXP],
                                                           op0=ALU.mult, op1=ALU.mult, accum_out=aux[:, 0:1]), "dve")
                                inc(v.scalar_tensor_tensor(out=scr16[:, 5 * NEXP:6 * NEXP], in0=halfbc[:], scalar=1.0,
                                                           in1=oh2s[:, st * NEXP:(st + 1) * NEXP],
                                                           op0=ALU.mult, op1=ALU.mult, accum_out=aux[:, 1:2]), "dve")
                                inc(v.scalar_tensor_tensor(out=scr16[:, 6 * NEXP:7 * NEXP], in0=parbc[:], scalar=1.0,
                                                           in1=oh1s[:, st * NEXP:(st + 1) * NEXP],
                                                           op0=ALU.mult, op1=ALU.mult, accum_out=aux[:, 2:3]), "dve")
                                ev = inc(v.scalar_tensor_tensor(out=scr16[:, 7 * NEXP:8 * NEXP], in0=parbc[:], scalar=1.0,
                                                                in1=oh2s[:, st * NEXP:(st + 1) * NEXP],
                                                                op0=ALU.mult, op1=ALU.mult, accum_out=aux[:, 3:4]), "dve")
                                inc(v.scalar_tensor_tensor(out=scr16[:, 8 * NEXP:9 * NEXP], in0=iota16[:], scalar=1.0,
                                                           in1=oh1s[:, st * NEXP:(st + 1) * NEXP],
                                                           op0=ALU.mult, op1=ALU.mult, accum_out=aux[:, 8:9]), "dve")
                                ev = inc(v.scalar_tensor_tensor(out=scr16[:, 9 * NEXP:10 * NEXP], in0=iota16[:], scalar=1.0,
                                                                in1=oh2s[:, st * NEXP:(st + 1) * NEXP],
                                                                op0=ALU.mult, op1=ALU.mult, accum_out=aux[:, 9:10]), "dve")
                                wge(v, ev)
                                for k in range(2):
                                    posk = pos1 if k == 0 else pos2
                                    wnk = w1c if k == 0 else w2c
                                    pck = bv1 if k == 0 else bv2
                                    hk = aux[:, k:k + 1]
                                    pk = aux[:, 2 + k:3 + k]
                                    idk = aux[:, 8 + k:9 + k]
                                    ev = inc(v.tensor_copy(auxu[:, 0:1], posk[:]), "dve")
                                    wge(v, ev)
                                    ev = inc(v.tensor_scalar(out=auxu[:, 1:2], in0=auxu[:, 0:1],
                                                             scalar1=127, scalar2=None,
                                                             op0=ALU.bitwise_and), "dve")
                                    wge(v, ev)
                                    ev = inc(v.tensor_copy(aux[:, 4:5], auxu[:, 1:2]), "dve")
                                    wge(v, ev)
                                    ev = inc(v.scalar_tensor_tensor(out=aux[:, 5:6], in0=aux[:, 4:5],
                                                                    scalar=-1.0, in1=posk[:],
                                                                    op0=ALU.mult, op1=ALU.add), "dve")
                                    wge(v, ev)
                                    ev = inc(v.tensor_scalar(out=aux[:, 5:6], in0=aux[:, 5:6],
                                                             scalar1=0.0078125, scalar2=None,
                                                             op0=ALU.mult), "dve")
                                    wge(v, ev)
                                    ev = inc(v.tensor_scalar(out=aux[:, 6:7], in0=idk, scalar1=5120.0,
                                                             scalar2=None, op0=ALU.mult), "dve")
                                    wge(v, ev)
                                    ev = inc(v.scalar_tensor_tensor(out=aux[:, 6:7], in0=aux[:, 4:5],
                                                                    scalar=40.0, in1=aux[:, 6:7],
                                                                    op0=ALU.mult, op1=ALU.add), "dve")
                                    wge(v, ev)
                                    ev = inc(v.tensor_tensor(out=aux[:, 6:7], in0=aux[:, 6:7],
                                                             in1=aux[:, 5:6], op=ALU.add), "dve")
                                    wge(v, ev)
                                    ev = inc(v.tensor_scalar(out=aux[:, 6:7], in0=aux[:, 6:7],
                                                             scalar1=wnk[:], scalar2=None, op0=ALU.mult), "dve")
                                    ev2 = inc(v.tensor_scalar(out=aux[:, 7:8], in0=wnk[:], scalar1=-1.0,
                                                              scalar2=1.0, op0=ALU.mult, op1=ALU.add), "dve")
                                    wge(v, ev, ev2)
                                    ev = inc(v.scalar_tensor_tensor(out=aux[:, 6:7], in0=oobcol[:],
                                                                    scalar=aux[:, 7:8], in1=aux[:, 6:7],
                                                                    op0=ALU.mult, op1=ALU.add), "dve")
                                    wge(v, ev)
                                    inc(v.tensor_copy(sidxu[:, st * 2 + k:st * 2 + k + 1], aux[:, 6:7]), "dve")
                                    ev = inc(v.scalar_tensor_tensor(out=aux[:, 4:5], in0=hk,
                                                                    scalar=5120.0, in1=pck[:],
                                                                    op0=ALU.mult, op1=ALU.add), "dve")
                                    wge(v, ev)
                                    ev = inc(v.scalar_tensor_tensor(out=aux[:, 4:5], in0=pk,
                                                                    scalar=float(8 * CAP), in1=aux[:, 4:5],
                                                                    op0=ALU.mult, op1=ALU.add), "dve")
                                    wge(v, ev)
                                    ev = inc(v.tensor_scalar(out=aux[:, 4:5], in0=aux[:, 4:5],
                                                             scalar1=wnk[:], scalar2=None,
                                                             op0=ALU.mult), "dve")
                                    ev2 = inc(v.tensor_scalar(out=aux[:, 5:6], in0=zrowcol[:],
                                                              scalar1=aux[:, 7:8], scalar2=None,
                                                              op0=ALU.mult), "dve")
                                    wge(v, ev, ev2)
                                    ev = inc(v.tensor_tensor(out=aux[:, 4:5], in0=aux[:, 4:5],
                                                             in1=aux[:, 5:6], op=ALU.add), "dve")
                                    wge(v, ev)
                                    ev = inc(v.tensor_copy(cidxu[:, st * 2 + k:st * 2 + k + 1],
                                                           aux[:, 4:5]), "dve")
                                    wge(v, ev)
                                ev = inc(v.tensor_scalar(out=aux[:, 4:5], in0=iotacol[:], scalar1=1.0,
                                                         scalar2=float(st * 128), op0=ALU.mult,
                                                         op1=ALU.add), "dve")
                                wge(v, ev)
                                ev = inc(v.tensor_tensor(out=aux[:, 4:5], in0=aux[:, 4:5], in1=rankvec[:],
                                                         op=ALU.add), "dve")
                                wge(v, ev)
                                inc(v.tensor_copy(tokidu[:, st:st + 1], aux[:, 4:5]), "dve")
                                wge(v, g_ev["cnt"])
                                inc(v.tensor_tensor(out=base[:], in0=base[:], in1=pCNT[0:1, 0:NEXP], op=ALU.add), "dve")
                            sec("vector", v_g)
                            ev_gate = now("dve")
                        ev_275_tail = now("dve")

                    def g_kick2(g):
                        wge(g, ev_275_tail)
                        wge(g, now("qp"))
                        for st in range(32):
                            for k in range(2):
                                if (st * 2 + k) % 4 == 0:
                                    wge(g, now("qp"))
                                iv = g.indirect_dma_start(
                                    out=t_c,
                                    out_offset=bass.IndirectOffsetOnAxis(
                                        ap=sidxu[:, st * 2 + k:st * 2 + k + 1], axis=0),
                                    in_=tokidu[:, st:st + 1], in_offset=None,
                                    bounds_check=bndregs["tc"], oob_is_err=False)
                                inc(iv, "qp", 16)
                        wge(g, now("qp"))
                        iv = g.collective_compute(
                            "ReduceScatter", ALU.add,
                            replica_groups=[[0, 1, 2, 3, 4, 5, 6, 7]],
                            ins=[t_c[:, :].opt()], outs=[t_own[:, :].opt()])
                        inc(iv, "cc")
                        wge(g, now("cc"))
                        inc(g.dma_start(out=tabf[:].rearrange("p (e j) -> p e j", j=40),
                                        in_=t_own.rearrange("(e p j) k -> p e (j k)", p=128, j=40)),
                            "qp", 16)
                    sec("gpsimd", g_kick2)
                    ev_tabf = now("qp")

                    def v_tab(v):
                        wge(v, ev_tabf)
                        inc(v.tensor_copy(tabu[:], tabf[:]), "dve")
                    sec("vector", v_tab)
                    ev_tabu = now("dve")

                    prev_up_tail = [None, ev_275_tail]
                    prev_dn_tail = [None, ev_275_tail]
                    prev_tr_tail = [None]
                    prev_store = [None]

                    def emit_mlp_tile(src_fm, ev_srcs, out_rows, out_q):
                        act_base = cnt["act"]
                        dve_base = cnt["dve"]
                        up_ev = {}

                        def t_up(t, up_tail=tuple(prev_up_tail), ev_srcs=tuple(ev_srcs),
                                 act_base=act_base, dve_base=dve_base):
                            wge(t, *ev_srcs)
                            wge(t, up_tail[0], up_tail[1])
                            for oc in range(11):
                                b0 = pU[(oc % 2) * 2]
                                b1 = pU[(oc % 2) * 2 + 1]
                                if oc >= 2:
                                    wge(t, ("act", act_base + oc - 1))
                                    wge(t, ("dve", dve_base + oc - 1))
                                for ic in range(8):
                                    iv = t.matmul(b0[:], wg3[:, ic * FF + oc * 128:ic * FF + oc * 128 + 128],
                                                  src_fm[:, ic * 512:(ic + 1) * 512],
                                                  start=(ic == 0), stop=(ic == 7))
                                up_ev[("g", oc)] = inc(iv, "pe")
                                for ic in range(8):
                                    iv = t.matmul(b1[:], wu3[:, ic * FF + oc * 128:ic * FF + oc * 128 + 128],
                                                  src_fm[:, ic * 512:(ic + 1) * 512],
                                                  start=(ic == 0), stop=(ic == 7))
                                up_ev[("u", oc)] = inc(iv, "pe")
                        sec("tensor", t_up)

                        def s_up(sc, dve_base=dve_base):
                            for oc in range(11):
                                b0 = pU[(oc % 2) * 2]
                                wge(sc, up_ev[("g", oc)])
                                if oc >= 2:
                                    wge(sc, ("dve", dve_base + oc - 1))
                                iv = sc.activation(out=silu_s[:, (oc % 2) * 512:(oc % 2) * 512 + 512],
                                                   in_=b0[:], func=AF.Silu)
                                up_ev[("s", oc)] = inc(iv, "act")
                        sec("scalar", s_up)
                        assert cnt["act"] == act_base + 11

                        def v_up(v):
                            for oc in range(11):
                                b1 = pU[(oc % 2) * 2 + 1]
                                wge(v, up_ev[("s", oc)], up_ev[("u", oc)])
                                iv = v.tensor_tensor(out=hid3[:, oc * 512:(oc + 1) * 512],
                                                     in0=silu_s[:, (oc % 2) * 512:(oc % 2) * 512 + 512],
                                                     in1=b1[:], op=ALU.mult)
                                inc(iv, "dve")
                        sec("vector", v_up)
                        assert cnt["dve"] == dve_base + 11
                        prev_up_tail[0] = now("act")
                        prev_up_tail[1] = now("dve")
                        ev_hid3 = now("dve")

                        dve_base = cnt["dve"]
                        act_base = cnt["act"]
                        dn_ev = {}

                        def t_dn(t, ev_hid3=ev_hid3, dve_base=dve_base, act_base=act_base,
                                 dn_tail=tuple(prev_dn_tail), pstore=prev_store[0]):
                            wge(t, ev_hid3)
                            wge(t, dn_tail[0], dn_tail[1], pstore)
                            pb = 0
                            for sx in range(NSUB):
                                for half in range(2):
                                    bank = pD0 if pb % 2 == 0 else pD1
                                    if pb >= 2:
                                        if pb % 2 == 0:
                                            wge(t, ("dve", dve_base + pb // 2))
                                        else:
                                            wge(t, ("act", act_base + (pb - 1) // 2))
                                    for oc in range(11):
                                        iv = t.matmul(bank[:],
                                                      hid3[:, oc * 512 + sx * 128:oc * 512 + sx * 128 + 128],
                                                      wd3[:, oc * D + half * 512:oc * D + half * 512 + 512],
                                                      start=(oc == 0), stop=(oc == 10))
                                    dn_ev[("m", pb)] = inc(iv, "pe")
                                    pb += 1
                        sec("tensor", t_dn)

                        def v_dn(v, dve_base=dve_base):
                            for sx in range(NSUB):
                                pb = sx * 2
                                wge(v, dn_ev[("m", pb)])
                                iv = v.tensor_copy(orow[:, sx * D:sx * D + 512], pD0[:])
                                inc(iv, "dve")
                        sec("vector", v_dn)

                        def s_dn(sc, act_base=act_base):
                            for sx in range(NSUB):
                                pb = sx * 2 + 1
                                wge(sc, dn_ev[("m", pb)])
                                iv = sc.copy(out=orow[:, sx * D + 512:(sx + 1) * D], in_=pD1[:])
                                inc(iv, "act")
                        sec("scalar", s_dn)
                        prev_dn_tail[0] = now("act")
                        prev_dn_tail[1] = now("dve")
                        ev_or = (now("dve"), now("act"))

                        hold = [None]

                        def s_st(sy, ev_or=ev_or):
                            wge(sy, ev_or[0], ev_or[1])
                            hold[0] = inc(sy.dma_start(
                                out=out_rows.rearrange("(s p) j -> p s j", p=128),
                                in_=orow[:].rearrange("p (s j) -> p s j", j=D)), out_q, 16)
                        sec("sync", s_st)
                        prev_store[0] = hold[0]
                        return hold[0]

                    ev_w_sh = now("qw")
                    ev_zg_free = None
                    for tau in range(NT):
                        holdz = [None]

                        def s_z(sy, tau=tau, ev_zg_free=ev_zg_free):
                            wge(sy, ev_zg_free)
                            iv = sy.dma_start(
                                out=zgf[:].rearrange("p (i t) -> p i t", t=512),
                                in_=sp_z2fm.rearrange("(i p) t -> p i t", p=128)[:, :, tau * NTOK:(tau + 1) * NTOK])
                            holdz[0] = inc(iv, "qs", 16)
                        sec("sync", s_z)
                        emit_mlp_tile(zgf, (holdz[0], ev_w_sh if tau == 0 else None),
                                      sp_acc[tau * NTOK:(tau + 1) * NTOK, :], "qs")
                        ev_zg_free = now("pe")

                    def g_shecho(g):
                        wge(g, now("qs"))
                        inc(g.engine_nop(), "pool")
                    sec("gpsimd", g_shecho)
                    ev_shared_done = now("pool")

                    ag_done = {}
                    for eps in range(2):
                        wsrcs = (wgo_h[eps], wuo_h[eps], wdo_h[eps])
                        ev_wread = now("pe")

                        def g_we(g, eps=eps, wsrcs=wsrcs, ev_wread=ev_wread):
                            wge(g, ev_wread)
                            inc(g.dma_start(out=wg3[:].rearrange("p (i j) -> p i j", j=FF),
                                            in_=wsrcs[0].rearrange("(i p) j -> p i j", p=128)), "qw", 16)
                            inc(g.dma_start(out=wu3[:].rearrange("p (i j) -> p i j", j=FF),
                                            in_=wsrcs[1].rearrange("(i p) j -> p i j", p=128)), "qw", 16)
                            inc(g.dma_start(out=wd3[:].rearrange("p (i j) -> p i j", j=D),
                                            in_=wsrcs[2].rearrange("(i p) j -> p i j", p=128)), "qw", 16)
                        sec("gpsimd", g_we)
                        ev_we = now("qw")

                        outX = outA if eps == 0 else outB
                        allX = all2[0:8 * CAP, :] if eps == 0 else all2[8 * CAP:16 * CAP, :]
                        last_store = None
                        for jt in range(10):
                            holdg = [None]

                            def g_rows(g, eps=eps, jt=jt):
                                wge(g, ev_tabu, ev_cc_z2)
                                wge(g, prev_tr_tail[0])
                                wge(g, now("qp"))
                                for i in range(4):
                                    col = eps * 40 + jt * 4 + i
                                    iv = g.indirect_dma_start(
                                        out=zgt[:, i * D:(i + 1) * D],
                                        out_offset=None,
                                        in_=z2_all,
                                        in_offset=bass.IndirectOffsetOnAxis(
                                            ap=tabu[:, col:col + 1], axis=0),
                                        bounds_check=bndregs["z2"], oob_is_err=False)
                                    inc(iv, "qp", 16)
                            sec("gpsimd", g_rows)
                            ev_rows = now("qp")

                            act_base = cnt["act"]
                            tr_ev = {}

                            def t_tr(t, ev_rows=ev_rows, act_base=act_base, eps=eps, jt=jt):
                                wge(t, ev_rows)
                                if jt == 0:
                                    wge(t, ev_cc_z2)
                                for ic in range(8):
                                    bank = pT0 if ic % 2 == 0 else pT1
                                    if ic >= 2:
                                        wge(t, ("act", act_base + ic - 1))
                                    for sx in range(NSUB):
                                        iv = t.transpose(bank[:, sx * 128:(sx + 1) * 128],
                                                         zgt[:, sx * D + ic * 128:sx * D + ic * 128 + 128],
                                                         identb[:])
                                    tr_ev[("t", ic)] = inc(iv, "pe")
                            sec("tensor", t_tr)
                            prev_tr_tail[0] = now("pe")

                            def s_tr(sc):
                                for ic in range(8):
                                    bank = pT0 if ic % 2 == 0 else pT1
                                    wge(sc, tr_ev[("t", ic)])
                                    iv = sc.copy(out=zgf[:, ic * 512:(ic + 1) * 512], in_=bank[:])
                                    inc(iv, "act")
                            sec("scalar", s_tr)
                            assert cnt["act"] == act_base + 8
                            ev_zgf = now("act")
                            st_ev = emit_mlp_tile(zgf, (ev_zgf, ev_we if jt == 0 else None),
                                                  outX[jt * 512:(jt + 1) * 512, :], "qs")
                            last_store = st_ev

                        def g_ag(g, eps=eps, allX=allX, outX=outX):
                            wge(g, now("qs"))
                            iv = g.collective_compute(
                                "AllGather", ALU.bypass,
                                replica_groups=[[0, 1, 2, 3, 4, 5, 6, 7]],
                                ins=[outX[:, :].opt()], outs=[allX.opt()])
                            inc(iv, "cc")
                        sec("gpsimd", g_ag)
                        ag_done[eps] = now("cc")

                    p3_end = {k: now(k) for k in sem}

                    if phases >= 4:
                        ev_buf_free = None
                        ev_y_prev = None
                        ev_gth_free = [None, None]
                        for g_i in range(32):
                            gbuf = gth[g_i % 2]
                            r0 = g_i * 128
                            f_ev = {}

                            def s_l4(sy, g_i=g_i, r0=r0, ev_buf_free=ev_buf_free):
                                wge(sy, ev_shared_done)
                                wge(sy, ev_buf_free)
                                inc(sy.dma_start(out=z2l[:], in_=sp_z2[r0:r0 + 128, :]), "qa", 16)
                                inc(sy.dma_start(out=x2l[:], in_=sp_x2[r0:r0 + 128, :]), "qa", 16)
                                f_ev["ld"] = inc(sy.dma_start(out=accL[:], in_=sp_acc[r0:r0 + 128, :]), "qa", 16)
                            sec("scalar", s_l4)

                            def g_c4(g, g_i=g_i, gbuf=gbuf, ev_gth_free=ev_gth_free[g_i % 2]):
                                wge(g, ag_done[0], ag_done[1])
                                wge(g, ev_gth_free)
                                wge(g, now("qp"))
                                for k in range(2):
                                    iv = g.indirect_dma_start(
                                        out=gbuf[:, k * D:(k + 1) * D],
                                        out_offset=None,
                                        in_=all2,
                                        in_offset=bass.IndirectOffsetOnAxis(
                                            ap=cidxu[:, g_i * 2 + k:g_i * 2 + k + 1],
                                            axis=0),
                                        bounds_check=bndregs["al"], oob_is_err=False)
                                    inc(iv, "qp", 16)
                                f_ev["gth"] = now("qp")
                            sec("gpsimd", g_c4)

                            def v_c4(v, g_i=g_i, gbuf=gbuf):
                                wge(v, f_ev["ld"], f_ev["gth"])
                                ev = inc(v.tensor_scalar(out=offn[:], in0=gbuf[:, 0:D],
                                                         scalar1=gwst[:, g_i * 2:g_i * 2 + 1],
                                                         scalar2=None, op0=ALU.mult), "dve")
                                wge(v, ev)
                                ev = inc(v.scalar_tensor_tensor(out=offn[:], in0=gbuf[:, D:2 * D],
                                                                scalar=gwst[:, g_i * 2 + 1:g_i * 2 + 2],
                                                                in1=offn[:], op0=ALU.mult, op1=ALU.add), "dve")
                                wge(v, ev)
                                ev = inc(v.tensor_tensor(out=offn[:], in0=offn[:], in1=z2l[:], op=ALU.add), "dve")
                                wge(v, ev)
                                ev = inc(v.tensor_tensor(out=offn[:], in0=offn[:], in1=accL[:], op=ALU.add), "dve")
                                f_ev["gthdone"] = ev
                                wge(v, ev)
                                v.bn_stats(bnst[:, 0:6], offn[:, 0:512])
                                ev = inc(v.bn_stats(bnst[:, 6:12], offn[:, 512:D]), "dve")
                                wge(v, ev)
                                f_ev["agg"] = inc(v.bn_aggr(mstat[:, 0:2], bnst[:, 0:12]), "dve")
                            sec("vector", v_c4)
                            ev_gth_free[g_i % 2] = f_ev["gthdone"]

                            def s_c4(sc, g_i=g_i):
                                wge(sc, f_ev["agg"])
                                iv = sc.activation(out=rsq[:, 0:1], in_=mstat[:, 1:2], func=AF.Sqrt,
                                                   bias=epsv[:])
                                f_ev["sq"] = inc(iv, "act")
                            sec("scalar", s_c4)

                            def v_f4(v, g_i=g_i, ev_y_prev=ev_y_prev):
                                wge(v, f_ev["sq"], ev_y_prev)
                                ev = inc(v.reciprocal(rsi[:, 0:1], rsq[:, 0:1]), "dve")
                                wge(v, ev)
                                ev = inc(v.tensor_scalar(out=outs4[:], in0=offn[:], scalar1=mstat[:, 0:1],
                                                         op0=ALU.subtract, scalar2=rsi[:, 0:1],
                                                         op1=ALU.mult), "dve")
                                wge(v, ev)
                                f_ev["out"] = inc(v.tensor_tensor(out=outs4[:], in0=outs4[:], in1=x2l[:],
                                                                  op=ALU.add), "dve")
                            sec("vector", v_f4)
                            ev_buf_free = f_ev["out"]

                            hold4 = [None]

                            def s_o4(sy, g_i=g_i, r0=r0):
                                wge(sy, f_ev["out"])
                                hold4[0] = inc(sy.dma_start(out=y_h[r0:r0 + 128, :], in_=outs4[:]), "qa", 16)
                            sec("scalar", s_o4)
                            ev_y_prev = hold4[0]

            def s_fin(sy):
                sy.wait_ge(sem["qs"], cnt["qs"])
                sy.wait_ge(sem["qp"], cnt["qp"])
                sy.wait_ge(sem["qa"], cnt["qa"])
                sy.wait_ge(sem["qw"], cnt["qw"])
            sec("sync", s_fin)

    return nc


def kernel(**inputs):
    x = np.ascontiguousarray(inputs["L0_attn_x"])
    taps = tuple(DEBUG_TAPS)
    nc = build_program(taps, BUILD_PHASES)
    bf = ml_dtypes.bfloat16
    shared = {
        "wq": inputs["L0_attn_wq"], "wk": inputs["L0_attn_wk"],
        "wv": inputs["L0_attn_wv"], "wo": inputs["L0_attn_wo"],
        "ln1g": inputs["L0_attn_ln1_g"], "ln1b": inputs["L0_attn_ln1_b"],
        "ln2g": inputs["L0_attn_ln2_g"], "ln2b": inputs["L0_attn_ln2_b"],
        "cw1": inputs["L0_attn_cms_w1"], "cb1": inputs["L0_attn_cms_b1"],
        "cw2": inputs["L0_attn_cms_w2"], "cb2": inputs["L0_attn_cms_b2"],
        "rw": inputs["L0_ffn_router_w"], "rb": inputs["L0_ffn_router_b"],
        "wg": inputs["L0_ffn_w_gate"], "wu": inputs["L0_ffn_w_up"],
        "wd": inputs["L0_ffn_w_down"],
        "swg": inputs["L0_ffn_sw_gate"], "swu": inputs["L0_ffn_sw_up"],
        "swd": inputs["L0_ffn_sw_down"],
        "identb_h": np.eye(128, dtype=bf),
        "identf_h": np.eye(128, dtype=np.float32),
        "maskt_h": np.triu(np.ones((128, 128), dtype=np.float32)),
        "trius_h": np.triu(np.ones((128, 128)), 1).astype(bf),
    }
    halfpar = np.concatenate([np.arange(16) // 2, np.arange(16) % 2]).astype(np.float32).reshape(1, 32)
    wg_full = inputs["L0_ffn_w_gate"]
    wu_full = inputs["L0_ffn_w_up"]
    wd_full = inputs["L0_ffn_w_down"]
    in_maps = []
    for c in range(8):
        m = dict(shared)
        m["x"] = np.ascontiguousarray(x[c])
        m["rankpre_h"] = (np.arange(8) < c).astype(np.float32).reshape(8, 1)
        m["rankvec_h"] = np.full((128, 1), c * 4096, np.float32)
        m["halfpar"] = halfpar
        m["wgo"] = np.ascontiguousarray(wg_full[2 * c:2 * c + 2])
        m["wuo"] = np.ascontiguousarray(wu_full[2 * c:2 * c + 2])
        m["wdo"] = np.ascontiguousarray(wd_full[2 * c:2 * c + 2])
        in_maps.append(m)
    res = run_bass_kernel_spmd(nc, in_maps, core_ids=list(range(8)))
    out = np.stack([np.asarray(res.results[c]["y"]) for c in range(8)]).astype(np.float32)
    if taps:
        kernel.last_results = res
    return out
